# version 07
# baseline (speedup 1.0000x reference)
import numpy as np
import concourse.bass as bass
import concourse.mybir as mybir
from concourse.bass_utils import run_bass_kernel_spmd

F32 = mybir.dt.float32
F16 = mybir.dt.float16
AF = mybir.ActivationFunctionType
OP = mybir.AluOpType

D = 1024
H = 4096
L = 2048
B = 16
NCORE = 8
SEQ_PER_CORE = 2
C = 256
KD = D // 128
KH = H // 128
TILES_PER_SEQ = L // C
NT = SEQ_PER_CORE * TILES_PER_SEQ
N = 16
EPS_LN = 1e-5
EPS_A = 1e-4
NSL = 4


def build_program(nt=NT, debug=False):
    nc = bass.Bass("TRN2", target_bir_lowering=False, detect_race_conditions=False)

    xT = nc.dram_tensor("xT", [D, SEQ_PER_CORE * L], F32, kind="ExternalInput").ap()
    w12s = nc.dram_tensor("w12s", [KH, 128, 2 * KD * 128], F32, kind="ExternalInput").ap()
    h1s = nc.dram_tensor("h1s", [KH, 128, KH * 128], F32, kind="ExternalInput").ap()
    h2s = nc.dram_tensor("h2s", [KH, 128, KH * 128], F32, kind="ExternalInput").ap()
    w3s = nc.dram_tensor("w3s", [KD, 128, KH * 128], F32, kind="ExternalInput").ap()
    wind_h = nc.dram_tensor("wind_h", [128, KD * N], F32, kind="ExternalInput").ap()
    winbe_h = nc.dram_tensor("winbe_h", [128, KD * N], F32, kind="ExternalInput").ap()
    winbo_h = nc.dram_tensor("winbo_h", [128, KD * N], F32, kind="ExternalInput").ap()
    wince_h = nc.dram_tensor("wince_h", [128, KD * N], F32, kind="ExternalInput").ap()
    winco_h = nc.dram_tensor("winco_h", [128, KD * N], F32, kind="ExternalInput").ap()
    bd_h = nc.dram_tensor("bd_h", [N, 1], F32, kind="ExternalInput").ap()
    bbe_h = nc.dram_tensor("bbe_h", [N, 1], F32, kind="ExternalInput").ap()
    bbo_h = nc.dram_tensor("bbo_h", [N, 1], F32, kind="ExternalInput").ap()
    bce_h = nc.dram_tensor("bce_h", [N, 1], F32, kind="ExternalInput").ap()
    bco_h = nc.dram_tensor("bco_h", [N, 1], F32, kind="ExternalInput").ap()
    wout_h = nc.dram_tensor("wout_h", [N, D], F32, kind="ExternalInput").ap()
    bout_h = nc.dram_tensor("bout_h", [128, KD], F32, kind="ExternalInput").ap()
    alog_h = nc.dram_tensor("alog_h", [N, 1], F32, kind="ExternalInput").ap()
    cw_h = nc.dram_tensor("cw_h", [128, KD * 4], F32, kind="ExternalInput").ap()
    cb_h = nc.dram_tensor("cb_h", [128, KD], F32, kind="ExternalInput").ap()
    s1_h = nc.dram_tensor("s1_h", [128, KH], F32, kind="ExternalInput").ap()
    s2_h = nc.dram_tensor("s2_h", [128, KH], F32, kind="ExternalInput").ap()
    sh1_h = nc.dram_tensor("sh1_h", [128, KH], F32, kind="ExternalInput").ap()
    sh2_h = nc.dram_tensor("sh2_h", [128, KH], F32, kind="ExternalInput").ap()
    s3_h = nc.dram_tensor("s3_h", [128, KD], F32, kind="ExternalInput").ap()
    yT = nc.dram_tensor("yT", [D, SEQ_PER_CORE * L], F32, kind="ExternalOutput").ap()
    w12q = nc.dram_tensor("w12q", [KH, 128, 2 * KD * 128], F16, kind="Internal").ap()
    h1q = nc.dram_tensor("h1q", [KH, 128, KH * 128], F16, kind="Internal").ap()
    h2q = nc.dram_tensor("h2q", [KH, 128, KH * 128], F16, kind="Internal").ap()
    w3q = nc.dram_tensor("w3q", [KD, 128, KH * 128], F16, kind="Internal").ap()

    S = nc.alloc_sbuf_tensor
    x_buf = [S(f"x{i}", [128, KD * C], F32).ap() for i in range(2)]
    xnhl = [S(f"xnhl{i}", [128, KD * 2 * C], F16).ap() for i in range(2)]
    ghl_h = S("ghl", [128, KH * 2 * C], F16)
    bhl_h = S("bhl", [128, KH * 2 * C], F16)
    ghl = ghl_h.ap()
    bhl = bhl_h.ap()
    wsl_h = [S(f"wsl{i}", [128, KH * 128], F16) for i in range(NSL)]
    wsl = [t.ap() for t in wsl_h]
    y_sb = S("y_sb", [128, KD * C], F32).ap()
    sq = S("sq", [128, KD * C], F32).ap()
    xc = S("xc", [128, KD * C], F32).ap()
    halo = S("halo", [128, C + 4], F32).ap()
    t1b = [S(f"t1b{i}", [128, C], F32).ap() for i in range(2)]
    chalf = [S(f"chalf{i}", [128, C], F32).ap() for i in range(2)]
    asum = [S(f"asum{i}", [128, C], F32).ap() for i in range(2)]
    bsum = [S(f"bsum{i}", [128, C], F32).ap() for i in range(2)]
    t2b = [S(f"t2b{i}", [128, C], F32).ap() for i in range(2)]
    btb = [S(f"btb{i}", [128, C], F32).ap() for i in range(2)]
    tmpc = S("tmpc", [128, C], F32).ap()
    stash = [S(f"stash{i}", [128, KD], F32).ap() for i in range(3)]
    musb = S("musb", [1, C], F32).ap()
    e2sb = S("e2sb", [1, C], F32).ap()
    varsb = S("varsb", [1, C], F32).ap()
    issb = S("issb", [1, C], F32).ap()
    ones_col = S("ones_col", [128, 1], F32).ap()
    ones_row = S("ones_row", [1, 128], F32).ap()
    win_d = S("win_d", [128, KD * N], F32).ap()
    win_b = S("win_b", [128, KD * N], F32).ap()
    win_c = S("win_c", [128, KD * N], F32).ap()
    win_t = S("win_t", [128, KD * N], F32).ap()
    wout_sb = S("wout_sb", [N, D], F32).ap()
    b_d = S("b_d", [N, 1], F32).ap()
    b_b = S("b_b", [N, 1], F32).ap()
    b_c = S("b_c", [N, 1], F32).ap()
    tb1 = S("tb1", [N, 1], F32).ap()
    tb2 = S("tb2", [N, 1], F32).ap()
    bout_sb = S("bout_sb", [128, KD], F32).ap()
    cw_sb = S("cw_sb", [128, KD * 4], F32).ap()
    cb_sb = S("cb_sb", [128, KD], F32).ap()
    s1_sb = S("s1_sb", [128, KH], F32).ap()
    s2_sb = S("s2_sb", [128, KH], F32).ap()
    sh1_sb = S("sh1_sb", [128, KH], F32).ap()
    sh2_sb = S("sh2_sb", [128, KH], F32).ap()
    s3_sb = S("s3_sb", [128, KD], F32).ap()
    alog_sb = S("alog_sb", [N, 1], F32).ap()
    a_sb = S("a_sb", [N, 1], F32).ap()
    mask_sb = S("mask_sb", [N, 1], F32).ap()
    inva_sb = S("inva_sb", [N, 1], F32).ap()
    tn1 = S("tn1", [N, 1], F32).ap()
    tn2 = S("tn2", [N, 1], F32).ap()
    d32 = S("d32", [N, C], F32).ap()
    b32 = S("b32", [N, C], F32).ap()
    c32 = S("c32", [N, C], F32).ap()
    sp1 = S("sp1", [N, C], F32).ap()
    sp2 = S("sp2", [N, C], F32).ap()
    sp3 = S("sp3", [N, C], F32).ap()
    dA = S("dA", [N, C], F32).ap()
    abar = S("abar", [N, C], F32).ap()
    tay = S("tay", [N, C], F32).ap()
    bbar = S("bbar", [N, C], F32).ap()
    hS = S("hS", [N, C], F32).ap()
    chS = S("chS", [N, C], F32).ap()
    carry = S("carry", [N, 1], F32).ap()
    CW = 1024
    NCB = 3
    csl = [S(f"csl{i}", [128, CW], F32).ap() for i in range(NCB)]
    cslq = [S(f"cslq{i}", [128, CW], F16).ap() for i in range(NCB)]

    P = nc.alloc_psum_tensor
    pa = [P(f"pa{i}", [128, 512], F32) for i in range(4)]
    pb = [P(f"pb{i}", [128, 512], F32) for i in range(2)]
    pc_t = P("pc", [128, 512], F32)
    pd_t = P("pd", [128, 512], F32)
    pa_ap = [t.ap() for t in pa]
    pb_ap = [t.ap() for t in pb]
    pc_ap = pc_t.ap()
    pd_ap = pd_t.ap()

    sem_names = ["s_wld", "s_xin", "s_y", "s_out", "s_sq", "s_stat", "s_stfr",
                 "s_var", "s_muis", "s_lnfr", "s_xc", "s_proj", "s_projf",
                 "s_ssm", "s_pa", "s_paf0", "s_paf1", "s_paf2", "s_paf3",
                 "s_pb", "s_pbf0", "s_pbf1", "s_ghl", "s_bhl", "s_chl",
                 "s_d1", "s_a1", "s_ha", "s_t1", "c_ld", "c_done_d", "c_done_g", "c_st", "p_sm"]
    sem = {n: nc.alloc_semaphore(name=n) for n in sem_names}

    ops = {"pe": [], "act": [], "dve": [], "gp": [], "sy": []}
    pro = {"pe": [], "act": [], "dve": [], "gp": [], "sy": []}

    def em(engine, fn):
        ops[engine].append(fn)

    def pem(engine, fn):
        pro[engine].append(fn)

    smalls = [(win_d, wind_h), (win_b, winbe_h), (win_c, wince_h),
              (b_d, bd_h), (tb1, bbe_h), (tb2, bbo_h),
              (tn1, bce_h), (tn2, bco_h), (wout_sb, wout_h),
              (bout_sb, bout_h), (alog_sb, alog_h), (cw_sb, cw_h),
              (cb_sb, cb_h), (s1_sb, s1_h), (s2_sb, s2_h), (sh1_sb, sh1_h),
              (sh2_sb, sh2_h), (s3_sb, s3_h)]
    for dst, src in smalls:
        pem("sy", lambda q, d=dst, s=src: q.dma_start(out=d, in_=s).then_inc(sem["p_sm"], 16))
    n_small = len(smalls) * 16

    def _win_odds(q):
        q.dma_start(out=win_t, in_=winbo_h).then_inc(sem["p_sm"], 16)
        q.wait_ge(sem["s_d1"], 4)
        q.dma_start(out=win_t, in_=winco_h).then_inc(sem["p_sm"], 16)
    pem("sy", _win_odds)


    def _prep_dve(v):
        v.wait_ge(sem["p_sm"], n_small)
        v.tensor_add(b_b, tb1, tb2)
        v.drain()
        v.tensor_scalar_mul(b_b, b_b, 0.5)
        v.tensor_add(b_c, tn1, tn2)
        v.drain()
        v.tensor_scalar_mul(b_c, b_c, 0.5)
        v.drain()
        v.memset(ones_col, 1.0)
        v.memset(ones_row, 1.0)
        v.memset(carry, 0.0)
        v.memset(halo, 0.0)
        for st_t in stash:
            v.memset(st_t, 0.0)
        v.wait_ge(sem["p_sm"], n_small + 16)
        v.tensor_add(win_b, win_b, win_t)
        v.tensor_scalar_mul(win_b, win_b, 0.5)
        v.engine_nop().then_inc(sem["s_d1"], 4)
        v.wait_ge(sem["p_sm"], n_small + 32)
        v.tensor_add(win_c, win_c, win_t)
        v.tensor_scalar_mul(win_c, win_c, 0.5)
        v.drain().then_inc(sem["s_a1"], 1)
    pem("dve", _prep_dve)

    def _prep_act(a):
        a.wait_ge(sem["s_a1"], 1)
        a.activation(out=a_sb, in_=alog_sb, func=AF.Exp)
        a.activation(out=a_sb, in_=a_sb, func=AF.Copy, scale=-1.0)
        a.sign(tn1, a_sb)
        a.activation(out=tn1, in_=tn1, func=AF.Copy, scale=EPS_A,
                     bias=EPS_A).then_inc(sem["s_d1"], 1)
    pem("act", _prep_act)

    def _prep_dve2(v):
        v.wait_ge(sem["s_d1"], 5)
        v.tensor_scalar_mul(tn2, a_sb, -1.0)
        v.drain()
        v.tensor_tensor(tn2, tn2, a_sb, OP.max)
        v.drain()
        v.tensor_scalar(mask_sb, tn2, -1e30, EPS_A * 1e30, OP.mult, OP.add)
        v.drain()
        v.tensor_relu(mask_sb, mask_sb)
        v.drain()
        v.tensor_scalar_min(mask_sb, mask_sb, 1.0)
        v.tensor_sub(tn1, tn1, a_sb)
        v.drain()
        v.tensor_mul(tn1, tn1, mask_sb)
        v.drain()
        v.tensor_add(tn2, tn1, a_sb)
        v.drain()
        v.reciprocal(inva_sb, tn2)
        v.drain()
        v.engine_nop().then_inc(sem["s_a1"], 1)
    pem("dve", _prep_dve2)

    cast_slabs = []
    for src3, dst3, mm, W in ((w12s, w12q, KH, 2 * KD * 128),
                              (h1s, h1q, KH, KH * 128),
                              (h2s, h2q, KH, KH * 128),
                              (w3s, w3q, KD, KH * 128)):
        for m in range(mm):
            for hf in range(W // CW):
                cast_slabs.append((src3[m][:, hf * CW:(hf + 1) * CW],
                                   dst3[m][:, hf * CW:(hf + 1) * CW], CW))
    NCS = len(cast_slabs)
    cast_load_fns = []
    for s, (src, dst, w) in enumerate(cast_slabs):
        b = s % NCB
        def _ldst(q, src=src, b=b, w=w, s=s):
            if s >= NCB:
                q.wait_ge(sem["c_done_d"], s - NCB + 1)
            q.dma_start(out=csl[b][:, 0:w], in_=src).then_inc(sem["c_ld"], 16)
            if s >= NCB:
                ps = s - NCB
                pdst, pw = cast_slabs[ps][1], cast_slabs[ps][2]
                q.dma_start(out=pdst, in_=cslq[ps % NCB][:, 0:pw]).then_inc(sem["c_st"], 16)
        cast_load_fns.append(_ldst)
        def _cast(g, b=b, w=w, s=s):
            g.wait_ge(sem["c_ld"], 16 * (s + 1))
            if s >= NCB:
                g.wait_ge(sem["c_st"], 16 * (s - NCB + 1))
            g.tensor_copy(cslq[b][:, 0:w], csl[b][:, 0:w]).then_inc(sem["c_done_d"], 1)
        pem("gp", _cast)
    def _cast_tail(q):
        for ps in range(NCS - NCB, NCS):
            pdst, pw = cast_slabs[ps][1], cast_slabs[ps][2]
            q.wait_ge(sem["c_done_d"], ps + 1)
            q.dma_start(out=pdst, in_=cslq[ps % NCB][:, 0:pw]).then_inc(sem["c_st"], 16)
    cast_load_fns.append(_cast_tail)

    GP_PER_IT = 2 * KH + KH + KH + KD

    def gidx(it, phase, m):
        base = (it - 1) * GP_PER_IT
        if phase == "a1":
            return base + 2 * m
        if phase == "a2":
            return base + 2 * m + 1
        if phase == "mm2":
            return base + 2 * KH + m
        if phase == "mm3":
            return base + 3 * KH + m
        return base + 4 * KH + m

    SL_PER_IT = 3 * KH + KD

    def slab_idx(it, phase, m):
        base = (it - 1) * SL_PER_IT
        off = {"w12": 0, "h1": KH, "h2": 2 * KH, "w3": 3 * KH}[phase]
        return base + off + m

    slab_tab = []
    for it in range(1, nt + 1):
        for m in range(KH):
            slab_tab.append((w12q[m], 2 * KD * 128, gidx(it, "a2", m)))
        for m in range(KH):
            slab_tab.append((h1q[m], KH * 128, gidx(it, "mm2", m)))
        for m in range(KH):
            slab_tab.append((h2q[m], KH * 128, gidx(it, "mm3", m)))
        for m in range(KD):
            slab_tab.append((w3q[m], KH * 128, gidx(it, "mm4", m)))

    def cst_gate(s_i):
        m = s_i % KH
        if s_i < KH:
            return 2 * m + 2
        if s_i < 2 * KH:
            return 64 + 4 * m + 4
        if s_i < 3 * KH:
            return 192 + 4 * m + 4
        return 320 + 4 * (s_i - 3 * KH) + 4

    emitted_cast = [0]
    for s_i, (src, w, _lg) in enumerate(slab_tab):
        if s_i < SL_PER_IT:
            need = min(cst_gate(s_i) + 3, len(cast_load_fns))
            while emitted_cast[0] < need:
                em("sy", cast_load_fns[emitted_cast[0]])
                emitted_cast[0] += 1
        def _ld(q, src=src, w=w, s_i=s_i):
            if s_i < SL_PER_IT:
                q.wait_ge(sem["c_st"], 16 * cst_gate(s_i))
            if s_i >= NSL:
                q.wait_ge(sem["s_pa"], slab_tab[s_i - NSL][2] + 1)
            q.dma_start(out=wsl[s_i % NSL][:, 0:w], in_=src).then_inc(sem["s_wld"], 16)
        em("sy", _ld)

    def pa_wait(pe, g):
        use = g // 4
        if use >= 1:
            pe.wait_ge(sem[f"s_paf{g % 4}"], use)

    n_iters = nt + 1
    for it in range(n_iters):
        i = it
        j = it - 1
        do_ln = i < nt
        do_ffn = j >= 0
        jb = j % 2
        ib = i % 2
        first_of_seq = do_ffn and (j % TILES_PER_SEQ) == 0
        last_of_seq = do_ffn and (j % TILES_PER_SEQ) == TILES_PER_SEQ - 1

        if do_ln:
            def _xdma(a, i=i, ib=ib):
                if i >= 2:
                    a.wait_ge(sem["s_out"], i - 1)
                a.dma_start(
                    out=x_buf[ib].rearrange("p (c t) -> p c t", t=C),
                    in_=xT.rearrange("(c p) t -> p c t", p=128)[:, :, i * C:(i + 1) * C],
                ).then_inc(sem["s_xin"], 16)
            em("act", _xdma)

        if do_ffn:
            def cp_wait(a, g):
                if g >= 2:
                    a.wait_ge(sem[f"s_paf{(g - 2) % 4}"], (g - 2) // 4 + 1)
            for m in range(KH):
                ga1 = gidx(it, "a1", m)
                ga2 = gidx(it, "a2", m)
                def _silu(a, m=m, ga1=ga1, ga2=ga2, it=it):
                    cp_wait(a, ga1)
                    a.wait_ge(sem["s_pa"], ga1 + 1)
                    a.copy(chalf[ga1 % 2], pa_ap[ga1 % 4][:, C:2 * C]
                           ).then_inc(sem["s_ha"], 1)
                    if m >= 2:
                        a.wait_ge(sem["s_ghl"], (it - 1) * KH + m - 1)
                    a.wait_ge(sem[f"s_paf{ga1 % 4}"], ga1 // 4 + 1)
                    a.activation(out=t1b[m % 2], in_=asum[m % 2],
                                 func=AF.Silu, scale=s1_sb[:, m:m + 1]
                                 ).then_inc(sem["s_t1"], 1)
                    cp_wait(a, ga2)
                    a.wait_ge(sem["s_pa"], ga2 + 1)
                    a.copy(chalf[ga2 % 2], pa_ap[ga2 % 4][:, C:2 * C]
                           ).then_inc(sem["s_ha"], 1)
                em("act", _silu)

        if do_ln:
            def _sqs(a, i=i, ib=ib):
                a.wait_ge(sem["s_xin"], 16 * (i + 1))
                if i >= 1:
                    a.wait_ge(sem["s_stat"], 4 * (i - 1) + 2)
                for c in range(KD):
                    a.activation(out=sq[:, c * C:(c + 1) * C],
                                 in_=x_buf[ib][:, c * C:(c + 1) * C],
                                 func=AF.Square)
                a.drain().then_inc(sem["s_sq"], 1)
            em("act", _sqs)
            def _mu(a, i=i):
                a.wait_ge(sem["s_stat"], 4 * i + 1)
                a.activation(out=musb, in_=pc_ap[0:1, 0:C], func=AF.Copy,
                             scale=1.0 / D).then_inc(sem["s_stfr"], 1)
                a.wait_ge(sem["s_stat"], 4 * i + 2)
                a.activation(out=e2sb, in_=pd_ap[0:1, 0:C], func=AF.Copy,
                             scale=1.0 / D).then_inc(sem["s_stfr"], 1)
            em("act", _mu)
            def _rs(a, i=i):
                a.wait_ge(sem["s_var"], i + 1)
                a.activation(out=issb, in_=varsb,
                             func=AF.Sqrt).then_inc(sem["s_muis"], 1)
            em("act", _rs)

        if do_ffn:
            def _ssm_act1(a, it=it):
                a.wait_ge(sem["s_d1"], 2 * it + 4)
                a.activation(out=sp2, in_=sp1, func=AF.Exp, scale=-1.0)
                a.activation(out=sp2, in_=sp2, func=AF.Ln,
                             bias=1.0).then_inc(sem["s_a1"], 1)
            def _ssm_act2(a, it=it):
                a.wait_ge(sem["s_d1"], 2 * it + 1 + 4)
                a.activation(out=abar, in_=dA, func=AF.Exp).then_inc(sem["s_a1"], 1)

            for m in range(KH):
                if m == 4:
                    em("act", _ssm_act1)
                g = gidx(it, "mm2", m)
                def _gelu(a, m=m, g=g, it=it):
                    cp_wait(a, g)
                    a.wait_ge(sem["s_pa"], g + 1)
                    a.copy(chalf[g % 2], pa_ap[g % 4][:, C:2 * C]
                           ).then_inc(sem["s_ha"], 1)
                    if m >= 2:
                        a.wait_ge(sem["s_bhl"], (it - 1) * KH + m - 1)
                    a.wait_ge(sem[f"s_paf{g % 4}"], g // 4 + 1)
                    a.activation(out=btb[m % 2], in_=bsum[m % 2],
                                 func=AF.Gelu, scale=sh1_sb[:, m:m + 1]
                                 ).then_inc(sem["s_t1"], 1)
                em("act", _gelu)
            em("act", _ssm_act2)
            for m in range(KH):
                g = gidx(it, "mm3", m)
                def _gelu3(a, m=m, g=g, it=it):
                    cp_wait(a, g)
                    a.wait_ge(sem["s_pa"], g + 1)
                    a.copy(chalf[g % 2], pa_ap[g % 4][:, C:2 * C]
                           ).then_inc(sem["s_ha"], 1)
                    if m == 0:
                        a.wait_ge(sem["s_bhl"], (it - 1) * KH + KH - 1)
                    elif m == 1:
                        a.wait_ge(sem["s_bhl"], (it - 1) * KH + KH)
                    else:
                        a.wait_ge(sem["s_chl"], (it - 1) * KH + m - 1)
                    a.wait_ge(sem[f"s_paf{g % 4}"], g // 4 + 1)
                    a.activation(out=btb[m % 2], in_=bsum[m % 2],
                                 func=AF.Gelu, scale=sh2_sb[:, m:m + 1]
                                 ).then_inc(sem["s_t1"], 1)
                em("act", _gelu3)

            for m in range(KD):
                g = gidx(it, "mm4", m)
                def _cp4(a, m=m, g=g):
                    cp_wait(a, g)
                    a.wait_ge(sem["s_pa"], g + 1)
                    a.copy(chalf[g % 2], pa_ap[g % 4][:, C:2 * C]
                           ).then_inc(sem["s_ha"], 1)
                em("act", _cp4)

            def _ydma(a, it=it, j=j):
                a.wait_ge(sem["s_out"], it)
                a.dma_start(
                    out=yT.rearrange("(c p) t -> p c t", p=128)[:, :, j * C:(j + 1) * C],
                    in_=y_sb.rearrange("p (c t) -> p c t", t=C),
                ).then_inc(sem["s_y"], 16)
            em("act", _ydma)

        if do_ffn:
            for m in range(KH):
                s_id = slab_idx(it, "w12", m)
                ga1 = gidx(it, "a1", m)
                ga2 = gidx(it, "a2", m)
                def _mm1(pe, m=m, s_id=s_id, ga1=ga1, ga2=ga2, jb=jb, it=it):
                    if m == 0:
                        pe.wait_ge(sem["s_lnfr"], it)
                    pe.wait_ge(sem["s_wld"], 16 * (s_id + 1))
                    pa_wait(pe, ga1)
                    for k in range(KD):
                        ins = pe.matmul(pa_ap[ga1 % 4][:, 0:2 * C],
                                        wsl[s_id % NSL][:, k * 128:(k + 1) * 128],
                                        xnhl[jb][:, k * 2 * C:(k + 1) * 2 * C],
                                        start=(k == 0), stop=(k == KD - 1))
                    ins.then_inc(sem["s_pa"], 1)
                    pa_wait(pe, ga2)
                    for k in range(KD):
                        ins = pe.matmul(pa_ap[ga2 % 4][:, 0:2 * C],
                                        wsl[s_id % NSL][:, (KD + k) * 128:(KD + k + 1) * 128],
                                        xnhl[jb][:, k * 2 * C:(k + 1) * 2 * C],
                                        start=(k == 0), stop=(k == KD - 1))
                    ins.then_inc(sem["s_pa"], 1)
                em("pe", _mm1)

        if do_ln:
            def _stats(pe, i=i, ib=ib):
                if i == 0:
                    pe.wait_ge(sem["p_sm"], n_small)
                    pe.wait_ge(sem["s_a1"], 2)
                if i >= 1:
                    pe.wait_ge(sem["s_lnfr"], i)
                    pe.wait_ge(sem["s_projf"], 3 * (i - 1))
                pe.wait_ge(sem["s_xin"], 16 * (i + 1))
                for c in range(KD):
                    ins = pe.matmul(pc_ap[0:1, 0:C], ones_col,
                                    x_buf[ib][:, c * C:(c + 1) * C],
                                    start=(c == 0), stop=(c == KD - 1))
                ins.then_inc(sem["s_stat"], 1)
                pe.wait_ge(sem["s_sq"], i + 1)
                for c in range(KD):
                    ins = pe.matmul(pd_ap[0:1, 0:C], ones_col,
                                    sq[:, c * C:(c + 1) * C],
                                    start=(c == 0), stop=(c == KD - 1))
                ins.then_inc(sem["s_stat"], 1)
                pe.wait_ge(sem["s_muis"], i + 1)
                pe.wait_ge(sem["s_stfr"], 2 * i + 1)
                pe.matmul(pc_ap[:, 0:C], ones_row, musb,
                          start=True, stop=True).then_inc(sem["s_stat"], 1)
                pe.wait_ge(sem["s_stfr"], 2 * i + 2)
                pe.matmul(pd_ap[:, 0:C], ones_row, issb,
                          start=True, stop=True).then_inc(sem["s_stat"], 1)
            em("pe", _stats)

        if do_ffn:
            def _proj(pe, it=it):
                pe.wait_ge(sem["s_lnfr"], min(it + 1, nt))
                for gi, wmat in enumerate((win_d, win_b, win_c)):
                    if gi > 0:
                        pe.wait_ge(sem["s_projf"], 3 * (it - 1) + gi)
                    for c in range(KD):
                        if gi == 0:
                            pe.wait_ge(sem["s_xc"], KD * (it - 1) + c + 1)
                        ins = pe.matmul(pc_ap[0:N, 0:C],
                                        wmat[:, c * N:(c + 1) * N],
                                        xc[:, c * C:(c + 1) * C],
                                        start=(c == 0), stop=(c == KD - 1))
                    ins.then_inc(sem["s_proj"], 1)

            for m in range(KH):
                if m == 8:
                    em("pe", _proj)
                s_id = slab_idx(it, "h1", m)
                g = gidx(it, "mm2", m)
                def _mm2(pe, m=m, s_id=s_id, g=g, it=it):
                    if m == 0:
                        pe.wait_ge(sem["s_ghl"], KH * it)
                    pe.wait_ge(sem["s_wld"], 16 * (s_id + 1))
                    pa_wait(pe, g)
                    for k in range(KH):
                        ins = pe.matmul(pa_ap[g % 4][:, 0:2 * C],
                                        wsl[s_id % NSL][:, k * 128:(k + 1) * 128],
                                        ghl[:, k * 2 * C:(k + 1) * 2 * C],
                                        start=(k == 0), stop=(k == KH - 1))
                    ins.then_inc(sem["s_pa"], 1)
                em("pe", _mm2)
            for m in range(KH):
                s_id = slab_idx(it, "h2", m)
                g = gidx(it, "mm3", m)
                def _mm3(pe, m=m, s_id=s_id, g=g, it=it):
                    if m == 0:
                        pe.wait_ge(sem["s_bhl"], KH * it)
                    pe.wait_ge(sem["s_wld"], 16 * (s_id + 1))
                    pa_wait(pe, g)
                    for k in range(KH):
                        ins = pe.matmul(pa_ap[g % 4][:, 0:2 * C],
                                        wsl[s_id % NSL][:, k * 128:(k + 1) * 128],
                                        bhl[:, k * 2 * C:(k + 1) * 2 * C],
                                        start=(k == 0), stop=(k == KH - 1))
                    ins.then_inc(sem["s_pa"], 1)
                em("pe", _mm3)
            for m in range(KD):
                s_id = slab_idx(it, "w3", m)
                g = gidx(it, "mm4", m)
                def _mm4(pe, m=m, s_id=s_id, g=g, it=it):
                    if m == 0:
                        pe.wait_ge(sem["s_chl"], KH * it)
                        pe.wait_ge(sem["s_ssm"], it)
                    pe.wait_ge(sem["s_wld"], 16 * (s_id + 1))
                    pa_wait(pe, g)
                    for k in range(KH):
                        ins = pe.matmul(pa_ap[g % 4][:, 0:2 * C],
                                        wsl[s_id % NSL][:, k * 128:(k + 1) * 128],
                                        ghl[:, k * 2 * C:(k + 1) * 2 * C],
                                        start=(k == 0), stop=(k == KH - 1))
                    ins.then_inc(sem["s_pa"], 1)
                    pbu = 4 * (it - 1) + m // 2
                    if pbu >= 1:
                        pe.wait_ge(sem[f"s_pbf{m % 2}"], pbu)
                    pe.matmul(pb_ap[m % 2][:, 0:C],
                              wout_sb[:, m * 128:(m + 1) * 128],
                              chS, start=True, stop=True).then_inc(sem["s_pb"], 1)
                em("pe", _mm4)

        if do_ffn:
            for m in range(KH):
                ga1 = gidx(it, "a1", m)
                ga2 = gidx(it, "a2", m)
                def _gev(v, m=m, ga1=ga1, ga2=ga2, it=it):
                    if m >= 2:
                        v.wait_ge(sem["s_t1"], 96 * (it - 1) + m - 1)
                    v.wait_ge(sem["s_ha"], ga1 + 1)
                    v.tensor_add(asum[m % 2], pa_ap[ga1 % 4][:, 0:C],
                                 chalf[ga1 % 2]).then_inc(sem[f"s_paf{ga1 % 4}"], 1)
                    v.wait_ge(sem["s_ha"], ga2 + 1)
                    v.tensor_add(t2b[m % 2], pa_ap[ga2 % 4][:, 0:C],
                                 chalf[ga2 % 2]).then_inc(sem[f"s_paf{ga2 % 4}"], 1)
                    v.wait_ge(sem["s_t1"], 96 * (it - 1) + m + 1)
                    v.scalar_tensor_tensor(tmpc, t1b[m % 2], s2_sb[:, m:m + 1],
                                           t2b[m % 2], OP.mult, OP.mult)
                    hi = ghl[:, m * 2 * C:m * 2 * C + C]
                    lo = ghl[:, m * 2 * C + C:(m + 1) * 2 * C]
                    v.tensor_copy(hi, tmpc)
                    v.tensor_tensor(t2b[m % 2], tmpc, hi, OP.subtract)
                    if m == KH - 1:
                        v.tensor_copy(lo, t2b[m % 2])
                        v.drain().then_inc(sem["s_ghl"], 1)
                    else:
                        v.tensor_copy(lo, t2b[m % 2]).then_inc(sem["s_ghl"], 1)
                em("dve", _gev)

        if do_ln:
            def _var(v, i=i):
                v.wait_ge(sem["s_stfr"], 2 * i + 2)
                v.tensor_mul(varsb, musb, musb)
                v.tensor_sub(varsb, e2sb, varsb)
                v.tensor_scalar_add(varsb, varsb, EPS_LN)
                v.reciprocal(varsb, varsb).then_inc(sem["s_var"], 1)
            em("dve", _var)
            def _xn(v, i=i, ib=ib):
                v.wait_ge(sem["s_stat"], 4 * i + 4)
                if i >= 2:
                    v.wait_ge(sem["s_pa"], gidx(i - 1, "a2", KH - 1) + 1)
                for c in range(KD):
                    xch = x_buf[ib][:, c * C:(c + 1) * C]
                    v.tensor_sub(tmpc, xch, pc_ap[:, 0:C])
                    v.tensor_mul(tmpc, tmpc, pd_ap[:, 0:C])
                    hi = xnhl[ib][:, c * 2 * C:c * 2 * C + C]
                    lo = xnhl[ib][:, c * 2 * C + C:(c + 1) * 2 * C]
                    v.tensor_copy(hi, tmpc)
                    v.tensor_copy(stash[i % 3][:, c:c + 1], tmpc[:, C - 1:C])
                    v.tensor_tensor(tmpc, tmpc, hi, OP.subtract)
                    v.tensor_copy(lo, tmpc)
                v.drain().then_inc(sem["s_lnfr"], 1)
            em("dve", _xn)

        if do_ffn:
            def _conv(v, it=it, j=j, jb=jb, ib=ib,
                      first=first_of_seq, last=last_of_seq):
                v.wait_ge(sem["s_proj"], 3 * (it - 1))
                for c in range(KD):
                    hi = xnhl[jb][:, c * 2 * C:c * 2 * C + C]
                    lo = xnhl[jb][:, c * 2 * C + C:(c + 1) * 2 * C]
                    if first:
                        v.memset(halo[:, 0:1], 0.0)
                    else:
                        v.tensor_copy(halo[:, 0:1], stash[(j - 1) % 3][:, c:c + 1])
                    if last:
                        v.memset(halo[:, 1 + C:3 + C], 0.0)
                    else:
                        nhl = xnhl[ib]
                        v.tensor_tensor(halo[:, 1 + C:3 + C],
                                        nhl[:, c * 2 * C:c * 2 * C + 2],
                                        nhl[:, c * 2 * C + C:c * 2 * C + C + 2],
                                        OP.add)
                    v.tensor_tensor(halo[:, 1:1 + C], hi, lo, OP.add)
                    xcc = xc[:, c * C:(c + 1) * C]
                    v.tensor_scalar_mul(xcc, halo[:, 0:C], cw_sb[:, c * 4:c * 4 + 1])
                    for k in range(1, 4):
                        v.scalar_tensor_tensor(xcc, halo[:, k:k + C],
                                               cw_sb[:, c * 4 + k:c * 4 + k + 1],
                                               xcc, OP.mult, OP.add)
                    if last:
                        v.memset(xc[:, (c + 1) * C - 1:(c + 1) * C], 0.0)
                        v.tensor_scalar_add(xc[:, c * C:(c + 1) * C - 1],
                                            xc[:, c * C:(c + 1) * C - 1],
                                            cb_sb[:, c:c + 1])
                    else:
                        v.tensor_scalar_add(xcc, xcc, cb_sb[:, c:c + 1])
                    v.drain().then_inc(sem["s_xc"], 1)
            em("dve", _conv)

            def _ssm1(v, it=it):
                v.wait_ge(sem["s_proj"], 3 * (it - 1) + 1)
                v.tensor_scalar_add(d32, pc_ap[0:N, 0:C],
                                    b_d).then_inc(sem["s_projf"], 1)
                v.tensor_scalar_mul(sp1, d32, -1.0)
                v.tensor_tensor(sp1, sp1, d32, OP.max).then_inc(sem["s_d1"], 1)
                v.tensor_relu(sp3, d32)
                v.wait_ge(sem["s_proj"], 3 * (it - 1) + 2)
                v.tensor_scalar_add(b32, pc_ap[0:N, 0:C],
                                    b_b).then_inc(sem["s_projf"], 1)
                v.wait_ge(sem["s_proj"], 3 * (it - 1) + 3)
                v.tensor_scalar_add(c32, pc_ap[0:N, 0:C],
                                    b_c).then_inc(sem["s_projf"], 1)
                v.wait_ge(sem["s_a1"], 2 * it + 1)
                v.tensor_add(sp3, sp3, sp2)
                v.tensor_scalar_mul(dA, sp3, a_sb).then_inc(sem["s_d1"], 1)
            def _ssm2(v, it=it, first=first_of_seq):
                v.tensor_mul(tay, dA, dA)
                v.tensor_scalar(sp3, dA, 0.5, 1.0, OP.mult, OP.add)
                v.scalar_tensor_tensor(tay, tay, 1.0 / 6.0, sp3, OP.mult, OP.add)
                v.wait_ge(sem["s_a1"], 2 * it + 2)
                v.tensor_scalar_add(sp3, abar, -1.0)
                v.tensor_scalar_mul(sp3, sp3, inva_sb)
                v.tensor_sub(tay, tay, sp3)
                v.scalar_tensor_tensor(sp3, tay, mask_sb, sp3, OP.mult, OP.add)
                v.tensor_mul(bbar, sp3, b32)
                if first:
                    v.memset(carry, 0.0)
                    v.drain()
                v.tensor_tensor_scan(hS, abar, bbar, carry, OP.mult, OP.add)
                v.drain()
                v.tensor_copy(carry, hS[:, C - 1:C])
                v.drain()
                if it >= 2:
                    v.wait_ge(sem["s_pb"], KD * (it - 1))
                v.tensor_mul(chS, c32, hS)
                v.drain().then_inc(sem["s_ssm"], 1)

            for m in range(KH):
                if m == 4:
                    em("dve", _ssm1)
                g = gidx(it, "mm2", m)
                def _bspl(v, m=m, g=g, it=it):
                    if m >= 2:
                        v.wait_ge(sem["s_t1"], 96 * (it - 1) + 32 + m - 1)
                    v.wait_ge(sem["s_ha"], g + 1)
                    v.tensor_add(bsum[m % 2], pa_ap[g % 4][:, 0:C],
                                 chalf[g % 2]).then_inc(sem[f"s_paf{g % 4}"], 1)
                    v.wait_ge(sem["s_t1"], 96 * (it - 1) + 32 + m + 1)
                    hi = bhl[:, m * 2 * C:m * 2 * C + C]
                    lo = bhl[:, m * 2 * C + C:(m + 1) * 2 * C]
                    v.tensor_copy(hi, btb[m % 2])
                    v.tensor_tensor(tmpc, btb[m % 2], hi, OP.subtract)
                    if m == KH - 1:
                        v.tensor_copy(lo, tmpc)
                        v.drain().then_inc(sem["s_bhl"], 1)
                    else:
                        v.tensor_copy(lo, tmpc).then_inc(sem["s_bhl"], 1)
                em("dve", _bspl)
            em("dve", _ssm2)
            for m in range(KH):
                g = gidx(it, "mm3", m)
                def _cspl(v, m=m, g=g, it=it):
                    u = 32 + m
                    v.wait_ge(sem["s_t1"], 96 * (it - 1) + 32 + u - 1)
                    v.wait_ge(sem["s_ha"], g + 1)
                    v.tensor_add(bsum[m % 2], pa_ap[g % 4][:, 0:C],
                                 chalf[g % 2]).then_inc(sem[f"s_paf{g % 4}"], 1)
                    v.wait_ge(sem["s_t1"], 96 * (it - 1) + 64 + m + 1)
                    hi = ghl[:, m * 2 * C:m * 2 * C + C]
                    lo = ghl[:, m * 2 * C + C:(m + 1) * 2 * C]
                    v.tensor_copy(hi, btb[m % 2])
                    v.tensor_tensor(tmpc, btb[m % 2], hi, OP.subtract)
                    if m == KH - 1:
                        v.tensor_copy(lo, tmpc)
                        v.drain().then_inc(sem["s_chl"], 1)
                    else:
                        v.tensor_copy(lo, tmpc).then_inc(sem["s_chl"], 1)
                em("dve", _cspl)
            for m in range(KD):
                g = gidx(it, "mm4", m)
                def _out(v, m=m, g=g, it=it, jb=jb):
                    if m == 0 and it >= 2:
                        v.wait_ge(sem["s_y"], 16 * (it - 1))
                    v.wait_ge(sem["s_ha"], g + 1)
                    ych = y_sb[:, m * C:(m + 1) * C]
                    v.tensor_add(ych, pa_ap[g % 4][:, 0:C], chalf[g % 2]
                                 ).then_inc(sem[f"s_paf{g % 4}"], 1)
                    v.tensor_scalar_mul(ych, ych, s3_sb[:, m:m + 1])
                    v.wait_ge(sem["s_pb"], KD * (it - 1) + m + 1)
                    v.tensor_add(ych, ych, pb_ap[m % 2][:, 0:C]
                                 ).then_inc(sem[f"s_pbf{m % 2}"], 1)
                    v.tensor_add(ych, ych, x_buf[jb][:, m * C:(m + 1) * C])
                    ins = v.tensor_scalar_add(ych, ych, bout_sb[:, m:m + 1])
                    if m == KD - 1:
                        ins.then_inc(sem["s_out"], 1)
                em("dve", _out)

    em("act", lambda a: a.wait_ge(sem["s_y"], 16 * nt))
    em("sy", lambda q: q.wait_ge(sem["s_y"], 16 * nt))

    dbg_tensors = {}
    if debug and nt == 0:
        dbg_tensors = dict(
            win_dd=win_d, win_bb=win_b, win_cc=win_c, a_s=a_sb,
            mask_s=mask_sb, inva_s=inva_sb, bd_s=b_d, bb_s=b_b, bc_s=b_c,
            cw_s=cw_sb, cb_s=cb_sb, s1_s=s1_sb, s3_s=s3_sb, wout_s=wout_sb,
            bout_s=bout_sb, tb1_s=tb1, tb2_s=tb2, tn1_s=tn1, tn2_s=tn2,
            csl0=csl[0], cslq0=cslq[0],
        )
    elif debug:
        dbg_tensors = dict(
            x0=x_buf[0], x1=x_buf[1], xnhl0=xnhl[0], xnhl1=xnhl[1], sq=sq,
            xc=xc, d32=d32, b32=b32, c32=c32, sp1=sp1, sp2=sp2,
            sp3=sp3, dA=dA, abar=abar, tay=tay, bbar=bbar, hS=hS, chS=chS,
            carry=carry, ghl=ghl, bhl=bhl, y_sbuf=y_sb, t1b0=t1b[0],
            t2b0=t2b[0], btb0=btb[0], musb=musb, e2sb=e2sb, varsb=varsb,
            issb=issb, stash0=stash[0], stash1=stash[1], stash2=stash[2],
            win_dd=win_d, win_bb=win_b, win_cc=win_c, a_s=a_sb,
            mask_s=mask_sb, inva_s=inva_sb, bd_s=b_d, bb_s=b_b, bc_s=b_c,
            cw_s=cw_sb, cb_s=cb_sb, s1_s=s1_sb, s3_s=s3_sb, wsl0=wsl[0],
            wsl1=wsl[1], wsl2=wsl[2], wsl3=wsl[3], wout_s=wout_sb,
            bout_s=bout_sb, w12q0=w12q[0], w12q7=w12q[7], h1q0=h1q[0],
            h1q31=h1q[31], h2q0=h2q[0], w3q0=w3q[0],
        )

    dbg_out = {}
    if debug:
        for k, ap in dbg_tensors.items():
            sh = [ap.shape[0], int(np.prod(list(ap.shape)[1:]))]
            dbg_out[k] = nc.dram_tensor("dbg_" + k, sh, ap.tensor.dtype,
                                        kind="ExternalOutput").ap()

    with nc.Block() as blk:
        @blk.sync
        def _(q):
            for fn in pro["sy"] + ops["sy"]:
                fn(q)
        @blk.gpsimd
        def _(g):
            for fn in pro["gp"] + ops["gp"]:
                fn(g)
        @blk.tensor
        def _(pe):
            for fn in pro["pe"] + ops["pe"]:
                fn(pe)
        @blk.vector
        def _(v):
            for fn in pro["dve"] + ops["dve"]:
                fn(v)
        @blk.scalar
        def _(a):
            for fn in pro["act"] + ops["act"]:
                fn(a)

    if debug:
        dsem = nc.alloc_semaphore(name="dsem")
        with nc.Block() as blk:
            @blk.sync
            def _(q):
                n = 0
                for k, ap in dbg_tensors.items():
                    q.dma_start(out=dbg_out[k], in_=ap).then_inc(dsem, 16)
                    n += 1
                q.wait_ge(dsem, 16 * n)

    return nc


def _host_prep(inputs):
    f = np.float32
    att_x = np.asarray(inputs["L0_attn_x"], dtype=f)
    w1T = np.ascontiguousarray(np.asarray(inputs["L0_ffn_w1_w"], dtype=f).T)
    w2T = np.ascontiguousarray(np.asarray(inputs["L0_ffn_w2_w"], dtype=f).T)
    h1T = np.ascontiguousarray(np.asarray(inputs["L0_ffn_h1_w"], dtype=f).T)
    h2T = np.ascontiguousarray(np.asarray(inputs["L0_ffn_h2_w"], dtype=f).T)
    w3T = np.ascontiguousarray(np.asarray(inputs["L0_ffn_w3_w"], dtype=f).T)

    def blk(wT, km, mm):
        return np.ascontiguousarray(
            wT.reshape(km, 128, mm, 128).transpose(2, 1, 0, 3).reshape(mm, 128, km * 128))

    w1b = blk(w1T, KD, KH)
    w2b = blk(w2T, KD, KH)
    w12s = np.ascontiguousarray(np.concatenate([w1b, w2b], axis=2))
    h1s = blk(h1T, KH, KH)
    h2s = blk(h2T, KH, KH)
    w3s = blk(w3T, KH, KD)

    win = np.asarray(inputs["L0_attn_Win"], dtype=f)
    def winblk(w):
        return np.ascontiguousarray(w.reshape(KD, 128, N).transpose(1, 0, 2).reshape(128, KD * N))
    wind_h = winblk(win[:, :N])
    winbe_h = winblk(win[:, N:3 * N:2])
    winbo_h = winblk(win[:, N + 1:3 * N:2])
    wince_h = winblk(win[:, 3 * N:5 * N:2])
    winco_h = winblk(win[:, 3 * N + 1:5 * N:2])
    b_in = np.asarray(inputs["L0_attn_b_in"], dtype=f)
    bd_h = np.ascontiguousarray(b_in[:N].reshape(N, 1))
    bbe_h = np.ascontiguousarray(b_in[N:3 * N:2].reshape(N, 1))
    bbo_h = np.ascontiguousarray(b_in[N + 1:3 * N:2].reshape(N, 1))
    bce_h = np.ascontiguousarray(b_in[3 * N:5 * N:2].reshape(N, 1))
    bco_h = np.ascontiguousarray(b_in[3 * N + 1:5 * N:2].reshape(N, 1))
    wout_h = np.ascontiguousarray(np.asarray(inputs["L0_attn_Wout"], dtype=f))
    bout_h = np.ascontiguousarray(np.asarray(inputs["L0_attn_b_out"], dtype=f).reshape(KD, 128).T)
    alog_h = np.ascontiguousarray(np.asarray(inputs["L0_attn_A_log"], dtype=f).reshape(N, 1))
    cw = np.asarray(inputs["L0_attn_conv_w"], dtype=f).reshape(D, 4)
    cw_h = np.ascontiguousarray(cw.reshape(KD, 128, 4).transpose(1, 0, 2).reshape(128, KD * 4))
    cb_h = np.ascontiguousarray(np.asarray(inputs["L0_attn_conv_b"], dtype=f).reshape(KD, 128).T)

    def sc(v, km):
        return np.ascontiguousarray(np.asarray(v, dtype=f).reshape(km, 128).T)

    shared = dict(
        w12s=w12s, h1s=h1s, h2s=h2s, w3s=w3s, wind_h=wind_h, winbe_h=winbe_h,
        winbo_h=winbo_h, wince_h=wince_h, winco_h=winco_h, bd_h=bd_h,
        bbe_h=bbe_h, bbo_h=bbo_h, bce_h=bce_h, bco_h=bco_h,
        wout_h=wout_h, bout_h=bout_h, alog_h=alog_h, cw_h=cw_h, cb_h=cb_h,
        s1_h=sc(inputs["L0_ffn_w1_s"], KH), s2_h=sc(inputs["L0_ffn_w2_s"], KH),
        sh1_h=sc(inputs["L0_ffn_h1_s"], KH), sh2_h=sc(inputs["L0_ffn_h2_s"], KH),
        s3_h=sc(inputs["L0_ffn_w3_s"], KD),
    )
    in_maps = []
    for c in range(NCORE):
        pair = att_x[SEQ_PER_CORE * c:SEQ_PER_CORE * (c + 1)]
        xTc = np.ascontiguousarray(pair.transpose(2, 0, 1).reshape(D, SEQ_PER_CORE * L))
        m = dict(shared)
        m["xT"] = xTc
        in_maps.append(m)
    return in_maps


def kernel(**inputs):
    in_maps = _host_prep(inputs)
    nc = build_program(NT)
    res = run_bass_kernel_spmd(nc, in_maps, core_ids=list(range(NCORE)))
    out = np.empty((B, L, D), dtype=np.float32)
    for c in range(NCORE):
        yT = res.results[c]["yT"]
        pair = yT.reshape(D, SEQ_PER_CORE, L).transpose(1, 2, 0)
        out[SEQ_PER_CORE * c:SEQ_PER_CORE * (c + 1)] = pair
    return out


# version 08
# speedup vs baseline: 1.0077x; 1.0077x over previous
import numpy as np
import concourse.bass as bass
import concourse.mybir as mybir
from concourse.bass_utils import run_bass_kernel_spmd

F32 = mybir.dt.float32
F16 = mybir.dt.float16
FP8 = mybir.dt.float8e4
AF = mybir.ActivationFunctionType
OP = mybir.AluOpType

D = 1024
H = 4096
L = 2048
B = 16
NCORE = 8
SEQ_PER_CORE = 2
C = 256
KD = D // 128
KH = H // 128
TILES_PER_SEQ = L // C
NT = SEQ_PER_CORE * TILES_PER_SEQ
N = 16
EPS_LN = 1e-5
EPS_A = 1e-4
NSL = 6


def build_program(nt=NT, debug=False):
    nc = bass.Bass("TRN2", target_bir_lowering=False, detect_race_conditions=False)

    xT = nc.dram_tensor("xT", [D, SEQ_PER_CORE * L], F32, kind="ExternalInput").ap()
    w12s = nc.dram_tensor("w12s", [KH, 128, 2 * KD * 128], F32, kind="ExternalInput").ap()
    h1s = nc.dram_tensor("h1s", [KH, 128, KH * 128], F32, kind="ExternalInput").ap()
    h2s = nc.dram_tensor("h2s", [KH, 128, KH * 128], F32, kind="ExternalInput").ap()
    w3s = nc.dram_tensor("w3s", [KD, 128, KH * 128], F32, kind="ExternalInput").ap()
    wind_h = nc.dram_tensor("wind_h", [128, KD * N], F32, kind="ExternalInput").ap()
    winbe_h = nc.dram_tensor("winbe_h", [128, KD * N], F32, kind="ExternalInput").ap()
    winbo_h = nc.dram_tensor("winbo_h", [128, KD * N], F32, kind="ExternalInput").ap()
    wince_h = nc.dram_tensor("wince_h", [128, KD * N], F32, kind="ExternalInput").ap()
    winco_h = nc.dram_tensor("winco_h", [128, KD * N], F32, kind="ExternalInput").ap()
    bd_h = nc.dram_tensor("bd_h", [N, 1], F32, kind="ExternalInput").ap()
    bbe_h = nc.dram_tensor("bbe_h", [N, 1], F32, kind="ExternalInput").ap()
    bbo_h = nc.dram_tensor("bbo_h", [N, 1], F32, kind="ExternalInput").ap()
    bce_h = nc.dram_tensor("bce_h", [N, 1], F32, kind="ExternalInput").ap()
    bco_h = nc.dram_tensor("bco_h", [N, 1], F32, kind="ExternalInput").ap()
    wout_h = nc.dram_tensor("wout_h", [N, D], F32, kind="ExternalInput").ap()
    bout_h = nc.dram_tensor("bout_h", [128, KD], F32, kind="ExternalInput").ap()
    alog_h = nc.dram_tensor("alog_h", [N, 1], F32, kind="ExternalInput").ap()
    cw_h = nc.dram_tensor("cw_h", [128, KD * 4], F32, kind="ExternalInput").ap()
    cb_h = nc.dram_tensor("cb_h", [128, KD], F32, kind="ExternalInput").ap()
    s1_h = nc.dram_tensor("s1_h", [128, KH], F32, kind="ExternalInput").ap()
    s2_h = nc.dram_tensor("s2_h", [128, KH], F32, kind="ExternalInput").ap()
    sh1_h = nc.dram_tensor("sh1_h", [128, KH], F32, kind="ExternalInput").ap()
    sh2_h = nc.dram_tensor("sh2_h", [128, KH], F32, kind="ExternalInput").ap()
    s3_h = nc.dram_tensor("s3_h", [128, KD], F32, kind="ExternalInput").ap()
    yT = nc.dram_tensor("yT", [D, SEQ_PER_CORE * L], F32, kind="ExternalOutput").ap()
    w12q = nc.dram_tensor("w12q", [KH, 128, 2 * KD * 128], FP8, kind="Internal").ap()
    h1q = nc.dram_tensor("h1q", [KH, 128, KH * 128], FP8, kind="Internal").ap()
    h2q = nc.dram_tensor("h2q", [KH, 128, KH * 128], FP8, kind="Internal").ap()
    w3q = nc.dram_tensor("w3q", [KD, 128, KH * 128], FP8, kind="Internal").ap()

    S = nc.alloc_sbuf_tensor
    x_buf = [S(f"x{i}", [128, KD * C], F32).ap() for i in range(2)]
    xnhl = [S(f"xnhl{i}", [128, KD * 2 * C], F16).ap() for i in range(2)]
    ghl_h = S("ghl", [128, KH * 2 * C], F16)
    bhl_h = S("bhl", [128, KH * 2 * C], F16)
    ghl = ghl_h.ap()
    bhl = bhl_h.ap()
    wsl_h = [S(f"wsl{i}", [128, KH * 128], FP8) for i in range(NSL)]
    wsl = [t.ap() for t in wsl_h]
    y_sb = S("y_sb", [128, KD * C], F32).ap()
    sq = S("sq", [128, KD * C], F32).ap()
    xc = S("xc", [128, KD * C], F32).ap()
    halo = S("halo", [128, C + 4], F32).ap()
    t1b = [S(f"t1b{i}", [128, C], F32).ap() for i in range(2)]
    chalf = [S(f"chalf{i}", [128, C], F32).ap() for i in range(2)]
    asum = [S(f"asum{i}", [128, C], F32).ap() for i in range(2)]
    bsum = [S(f"bsum{i}", [128, C], F32).ap() for i in range(2)]
    t2b = [S(f"t2b{i}", [128, C], F32).ap() for i in range(2)]
    btb = [S(f"btb{i}", [128, C], F32).ap() for i in range(2)]
    tmpc = S("tmpc", [128, C], F32).ap()
    stash = [S(f"stash{i}", [128, KD], F32).ap() for i in range(3)]
    musb = S("musb", [1, C], F32).ap()
    e2sb = S("e2sb", [1, C], F32).ap()
    varsb = S("varsb", [1, C], F32).ap()
    issb = S("issb", [1, C], F32).ap()
    ones_col = S("ones_col", [128, 1], F32).ap()
    ones_row = S("ones_row", [1, 128], F32).ap()
    win_d = S("win_d", [128, KD * N], F32).ap()
    win_b = S("win_b", [128, KD * N], F32).ap()
    win_c = S("win_c", [128, KD * N], F32).ap()
    win_t = S("win_t", [128, KD * N], F32).ap()
    wout_sb = S("wout_sb", [N, D], F32).ap()
    b_d = S("b_d", [N, 1], F32).ap()
    b_b = S("b_b", [N, 1], F32).ap()
    b_c = S("b_c", [N, 1], F32).ap()
    tb1 = S("tb1", [N, 1], F32).ap()
    tb2 = S("tb2", [N, 1], F32).ap()
    bout_sb = S("bout_sb", [128, KD], F32).ap()
    cw_sb = S("cw_sb", [128, KD * 4], F32).ap()
    cb_sb = S("cb_sb", [128, KD], F32).ap()
    s1_sb = S("s1_sb", [128, KH], F32).ap()
    s2_sb = S("s2_sb", [128, KH], F32).ap()
    sh1_sb = S("sh1_sb", [128, KH], F32).ap()
    sh2_sb = S("sh2_sb", [128, KH], F32).ap()
    s3_sb = S("s3_sb", [128, KD], F32).ap()
    alog_sb = S("alog_sb", [N, 1], F32).ap()
    a_sb = S("a_sb", [N, 1], F32).ap()
    mask_sb = S("mask_sb", [N, 1], F32).ap()
    inva_sb = S("inva_sb", [N, 1], F32).ap()
    tn1 = S("tn1", [N, 1], F32).ap()
    tn2 = S("tn2", [N, 1], F32).ap()
    d32 = S("d32", [N, C], F32).ap()
    b32 = S("b32", [N, C], F32).ap()
    c32 = S("c32", [N, C], F32).ap()
    sp1 = S("sp1", [N, C], F32).ap()
    sp2 = S("sp2", [N, C], F32).ap()
    sp3 = S("sp3", [N, C], F32).ap()
    dA = S("dA", [N, C], F32).ap()
    abar = S("abar", [N, C], F32).ap()
    tay = S("tay", [N, C], F32).ap()
    bbar = S("bbar", [N, C], F32).ap()
    hS = S("hS", [N, C], F32).ap()
    chS = S("chS", [N, C], F32).ap()
    carry = S("carry", [N, 1], F32).ap()
    CW = 1024
    NCB = 3
    csl = [S(f"csl{i}", [128, CW], F32).ap() for i in range(NCB)]
    cslq = [S(f"cslq{i}", [128, CW], FP8).ap() for i in range(NCB)]

    P = nc.alloc_psum_tensor
    pa = [P(f"pa{i}", [128, 512], F32) for i in range(4)]
    pb = [P(f"pb{i}", [128, 512], F32) for i in range(2)]
    pc_t = P("pc", [128, 512], F32)
    pd_t = P("pd", [128, 512], F32)
    pa_ap = [t.ap() for t in pa]
    pb_ap = [t.ap() for t in pb]
    pc_ap = pc_t.ap()
    pd_ap = pd_t.ap()

    sem_names = ["s_wld", "s_xin", "s_y", "s_out", "s_sq", "s_stat", "s_stfr",
                 "s_var", "s_muis", "s_lnfr", "s_xc", "s_proj", "s_projf",
                 "s_ssm", "s_pa", "s_paf0", "s_paf1", "s_paf2", "s_paf3",
                 "s_pb", "s_pbf0", "s_pbf1", "s_ghl", "s_bhl", "s_chl",
                 "s_d1", "s_a1", "s_ha", "s_t1", "c_ld", "c_done_d", "c_done_g", "c_st", "p_sm"]
    sem = {n: nc.alloc_semaphore(name=n) for n in sem_names}

    ops = {"pe": [], "act": [], "dve": [], "gp": [], "sy": []}
    pro = {"pe": [], "act": [], "dve": [], "gp": [], "sy": []}

    def em(engine, fn):
        ops[engine].append(fn)

    def pem(engine, fn):
        pro[engine].append(fn)

    smalls = [(win_d, wind_h), (win_b, winbe_h), (win_c, wince_h),
              (b_d, bd_h), (tb1, bbe_h), (tb2, bbo_h),
              (tn1, bce_h), (tn2, bco_h), (wout_sb, wout_h),
              (bout_sb, bout_h), (alog_sb, alog_h), (cw_sb, cw_h),
              (cb_sb, cb_h), (s1_sb, s1_h), (s2_sb, s2_h), (sh1_sb, sh1_h),
              (sh2_sb, sh2_h), (s3_sb, s3_h)]
    for dst, src in smalls:
        pem("sy", lambda q, d=dst, s=src: q.dma_start(out=d, in_=s).then_inc(sem["p_sm"], 16))
    n_small = len(smalls) * 16

    def _win_odds(q):
        q.dma_start(out=win_t, in_=winbo_h).then_inc(sem["p_sm"], 16)
        q.wait_ge(sem["s_d1"], 4)
        q.dma_start(out=win_t, in_=winco_h).then_inc(sem["p_sm"], 16)
    pem("sy", _win_odds)


    def _prep_dve(v):
        v.wait_ge(sem["p_sm"], n_small)
        v.tensor_add(b_b, tb1, tb2)
        v.drain()
        v.tensor_scalar_mul(b_b, b_b, 0.5)
        v.tensor_add(b_c, tn1, tn2)
        v.drain()
        v.tensor_scalar_mul(b_c, b_c, 0.5)
        v.drain()
        v.memset(ones_col, 1.0)
        v.memset(ones_row, 1.0)
        v.memset(carry, 0.0)
        v.memset(halo, 0.0)
        for st_t in stash:
            v.memset(st_t, 0.0)
        v.wait_ge(sem["p_sm"], n_small + 16)
        v.tensor_add(win_b, win_b, win_t)
        v.tensor_scalar_mul(win_b, win_b, 0.5)
        v.engine_nop().then_inc(sem["s_d1"], 4)
        v.wait_ge(sem["p_sm"], n_small + 32)
        v.tensor_add(win_c, win_c, win_t)
        v.tensor_scalar_mul(win_c, win_c, 0.5)
        v.drain().then_inc(sem["s_a1"], 1)
    pem("dve", _prep_dve)

    def _prep_act(a):
        a.wait_ge(sem["s_a1"], 1)
        a.activation(out=a_sb, in_=alog_sb, func=AF.Exp)
        a.activation(out=a_sb, in_=a_sb, func=AF.Copy, scale=-1.0)
        a.sign(tn1, a_sb)
        a.activation(out=tn1, in_=tn1, func=AF.Copy, scale=EPS_A,
                     bias=EPS_A).then_inc(sem["s_d1"], 1)
    pem("act", _prep_act)

    def _prep_dve2(v):
        v.wait_ge(sem["s_d1"], 5)
        v.tensor_scalar_mul(tn2, a_sb, -1.0)
        v.drain()
        v.tensor_tensor(tn2, tn2, a_sb, OP.max)
        v.drain()
        v.tensor_scalar(mask_sb, tn2, -1e30, EPS_A * 1e30, OP.mult, OP.add)
        v.drain()
        v.tensor_relu(mask_sb, mask_sb)
        v.drain()
        v.tensor_scalar_min(mask_sb, mask_sb, 1.0)
        v.tensor_sub(tn1, tn1, a_sb)
        v.drain()
        v.tensor_mul(tn1, tn1, mask_sb)
        v.drain()
        v.tensor_add(tn2, tn1, a_sb)
        v.drain()
        v.reciprocal(inva_sb, tn2)
        v.drain()
        v.engine_nop().then_inc(sem["s_a1"], 1)
    pem("dve", _prep_dve2)

    cast_slabs = []
    for src3, dst3, mm, W in ((w12s, w12q, KH, 2 * KD * 128),
                              (h1s, h1q, KH, KH * 128),
                              (h2s, h2q, KH, KH * 128),
                              (w3s, w3q, KD, KH * 128)):
        for m in range(mm):
            for hf in range(W // CW):
                cast_slabs.append((src3[m][:, hf * CW:(hf + 1) * CW],
                                   dst3[m][:, hf * CW:(hf + 1) * CW], CW))
    NCS = len(cast_slabs)
    cast_load_fns = []
    for s, (src, dst, w) in enumerate(cast_slabs):
        b = s % NCB
        def _ldst(q, src=src, b=b, w=w, s=s):
            if s >= NCB:
                q.wait_ge(sem["c_done_d"], s - NCB + 1)
            q.dma_start(out=csl[b][:, 0:w], in_=src).then_inc(sem["c_ld"], 16)
            if s >= NCB:
                ps = s - NCB
                pdst, pw = cast_slabs[ps][1], cast_slabs[ps][2]
                q.dma_start(out=pdst, in_=cslq[ps % NCB][:, 0:pw]).then_inc(sem["c_st"], 16)
        cast_load_fns.append(_ldst)
        def _cast(g, b=b, w=w, s=s):
            g.wait_ge(sem["c_ld"], 16 * (s + 1))
            if s >= NCB:
                g.wait_ge(sem["c_st"], 16 * (s - NCB + 1))
            g.tensor_copy(cslq[b][:, 0:w], csl[b][:, 0:w]).then_inc(sem["c_done_d"], 1)
        pem("gp", _cast)
    def _cast_tail(q):
        for ps in range(NCS - NCB, NCS):
            pdst, pw = cast_slabs[ps][1], cast_slabs[ps][2]
            q.wait_ge(sem["c_done_d"], ps + 1)
            q.dma_start(out=pdst, in_=cslq[ps % NCB][:, 0:pw]).then_inc(sem["c_st"], 16)
    cast_load_fns.append(_cast_tail)

    GP_PER_IT = 2 * KH + KH + KH + KD

    def gidx(it, phase, m):
        base = (it - 1) * GP_PER_IT
        if phase == "a1":
            return base + 2 * m
        if phase == "a2":
            return base + 2 * m + 1
        if phase == "mm2":
            return base + 2 * KH + m
        if phase == "mm3":
            return base + 3 * KH + m
        return base + 4 * KH + m

    SL_PER_IT = 3 * KH + KD

    def slab_idx(it, phase, m):
        base = (it - 1) * SL_PER_IT
        off = {"w12": 0, "h1": KH, "h2": 2 * KH, "w3": 3 * KH}[phase]
        return base + off + m

    slab_tab = []
    for it in range(1, nt + 1):
        for m in range(KH):
            slab_tab.append((w12q[m], 2 * KD * 128, gidx(it, "a2", m)))
        for m in range(KH):
            slab_tab.append((h1q[m], KH * 128, gidx(it, "mm2", m)))
        for m in range(KH):
            slab_tab.append((h2q[m], KH * 128, gidx(it, "mm3", m)))
        for m in range(KD):
            slab_tab.append((w3q[m], KH * 128, gidx(it, "mm4", m)))

    def cst_gate(s_i):
        m = s_i % KH
        if s_i < KH:
            return 2 * m + 2
        if s_i < 2 * KH:
            return 64 + 4 * m + 4
        if s_i < 3 * KH:
            return 192 + 4 * m + 4
        return 320 + 4 * (s_i - 3 * KH) + 4

    emitted_cast = [0]
    for s_i, (src, w, _lg) in enumerate(slab_tab):
        if s_i < SL_PER_IT:
            need = min(cst_gate(s_i) + 3, len(cast_load_fns))
            while emitted_cast[0] < need:
                em("sy", cast_load_fns[emitted_cast[0]])
                emitted_cast[0] += 1
        def _ld(q, src=src, w=w, s_i=s_i):
            if s_i < SL_PER_IT:
                q.wait_ge(sem["c_st"], 16 * cst_gate(s_i))
            if s_i >= NSL:
                q.wait_ge(sem["s_pa"], slab_tab[s_i - NSL][2] + 1)
            q.dma_start(out=wsl[s_i % NSL][:, 0:w], in_=src).then_inc(sem["s_wld"], 16)
        em("sy", _ld)

    def pa_wait(pe, g):
        use = g // 4
        if use >= 1:
            pe.wait_ge(sem[f"s_paf{g % 4}"], use)

    n_iters = nt + 1
    for it in range(n_iters):
        i = it
        j = it - 1
        do_ln = i < nt
        do_ffn = j >= 0
        jb = j % 2
        ib = i % 2
        first_of_seq = do_ffn and (j % TILES_PER_SEQ) == 0
        last_of_seq = do_ffn and (j % TILES_PER_SEQ) == TILES_PER_SEQ - 1

        if do_ln:
            def _xdma(a, i=i, ib=ib):
                if i >= 2:
                    a.wait_ge(sem["s_out"], i - 1)
                a.dma_start(
                    out=x_buf[ib].rearrange("p (c t) -> p c t", t=C),
                    in_=xT.rearrange("(c p) t -> p c t", p=128)[:, :, i * C:(i + 1) * C],
                ).then_inc(sem["s_xin"], 16)
            em("act", _xdma)

        if do_ffn:
            def cp_wait(a, g):
                if g >= 2:
                    a.wait_ge(sem[f"s_paf{(g - 2) % 4}"], (g - 2) // 4 + 1)
            for m in range(KH):
                ga1 = gidx(it, "a1", m)
                ga2 = gidx(it, "a2", m)
                def _silu(a, m=m, ga1=ga1, ga2=ga2, it=it):
                    cp_wait(a, ga1)
                    a.wait_ge(sem["s_pa"], ga1 + 1)
                    a.copy(chalf[ga1 % 2], pa_ap[ga1 % 4][:, C:2 * C]
                           ).then_inc(sem["s_ha"], 1)
                    if m >= 2:
                        a.wait_ge(sem["s_ghl"], (it - 1) * KH + m - 1)
                    a.wait_ge(sem[f"s_paf{ga1 % 4}"], ga1 // 4 + 1)
                    a.activation(out=t1b[m % 2], in_=asum[m % 2],
                                 func=AF.Silu, scale=s1_sb[:, m:m + 1]
                                 ).then_inc(sem["s_t1"], 1)
                    cp_wait(a, ga2)
                    a.wait_ge(sem["s_pa"], ga2 + 1)
                    a.copy(chalf[ga2 % 2], pa_ap[ga2 % 4][:, C:2 * C]
                           ).then_inc(sem["s_ha"], 1)
                em("act", _silu)

        if do_ln:
            def _sqs(a, i=i, ib=ib):
                a.wait_ge(sem["s_xin"], 16 * (i + 1))
                if i >= 1:
                    a.wait_ge(sem["s_stat"], 4 * (i - 1) + 2)
                for c in range(KD):
                    a.activation(out=sq[:, c * C:(c + 1) * C],
                                 in_=x_buf[ib][:, c * C:(c + 1) * C],
                                 func=AF.Square)
                a.drain().then_inc(sem["s_sq"], 1)
            em("act", _sqs)
            def _mu(a, i=i):
                a.wait_ge(sem["s_stat"], 4 * i + 1)
                a.activation(out=musb, in_=pc_ap[0:1, 0:C], func=AF.Copy,
                             scale=1.0 / D).then_inc(sem["s_stfr"], 1)
                a.wait_ge(sem["s_stat"], 4 * i + 2)
                a.activation(out=e2sb, in_=pd_ap[0:1, 0:C], func=AF.Copy,
                             scale=1.0 / D).then_inc(sem["s_stfr"], 1)
            em("act", _mu)
            def _rs(a, i=i):
                a.wait_ge(sem["s_var"], i + 1)
                a.activation(out=issb, in_=varsb,
                             func=AF.Sqrt).then_inc(sem["s_muis"], 1)
            em("act", _rs)

        if do_ffn:
            def _ssm_act1(a, it=it):
                a.wait_ge(sem["s_d1"], 2 * it + 4)
                a.activation(out=sp2, in_=sp1, func=AF.Exp, scale=-1.0)
                a.activation(out=sp2, in_=sp2, func=AF.Ln,
                             bias=1.0).then_inc(sem["s_a1"], 1)
            def _ssm_act2(a, it=it):
                a.wait_ge(sem["s_d1"], 2 * it + 1 + 4)
                a.activation(out=abar, in_=dA, func=AF.Exp).then_inc(sem["s_a1"], 1)

            for m in range(KH):
                if m == 4:
                    em("act", _ssm_act1)
                g = gidx(it, "mm2", m)
                def _gelu(a, m=m, g=g, it=it):
                    cp_wait(a, g)
                    a.wait_ge(sem["s_pa"], g + 1)
                    a.copy(chalf[g % 2], pa_ap[g % 4][:, C:2 * C]
                           ).then_inc(sem["s_ha"], 1)
                    if m >= 2:
                        a.wait_ge(sem["s_bhl"], (it - 1) * KH + m - 1)
                    a.wait_ge(sem[f"s_paf{g % 4}"], g // 4 + 1)
                    a.activation(out=btb[m % 2], in_=bsum[m % 2],
                                 func=AF.Gelu, scale=sh1_sb[:, m:m + 1]
                                 ).then_inc(sem["s_t1"], 1)
                em("act", _gelu)
            em("act", _ssm_act2)
            for m in range(KH):
                g = gidx(it, "mm3", m)
                def _gelu3(a, m=m, g=g, it=it):
                    cp_wait(a, g)
                    a.wait_ge(sem["s_pa"], g + 1)
                    a.copy(chalf[g % 2], pa_ap[g % 4][:, C:2 * C]
                           ).then_inc(sem["s_ha"], 1)
                    if m == 0:
                        a.wait_ge(sem["s_bhl"], (it - 1) * KH + KH - 1)
                    elif m == 1:
                        a.wait_ge(sem["s_bhl"], (it - 1) * KH + KH)
                    else:
                        a.wait_ge(sem["s_chl"], (it - 1) * KH + m - 1)
                    a.wait_ge(sem[f"s_paf{g % 4}"], g // 4 + 1)
                    a.activation(out=btb[m % 2], in_=bsum[m % 2],
                                 func=AF.Gelu, scale=sh2_sb[:, m:m + 1]
                                 ).then_inc(sem["s_t1"], 1)
                em("act", _gelu3)

            for m in range(KD):
                g = gidx(it, "mm4", m)
                def _cp4(a, m=m, g=g):
                    cp_wait(a, g)
                    a.wait_ge(sem["s_pa"], g + 1)
                    a.copy(chalf[g % 2], pa_ap[g % 4][:, C:2 * C]
                           ).then_inc(sem["s_ha"], 1)
                em("act", _cp4)

            def _ydma(a, it=it, j=j):
                a.wait_ge(sem["s_out"], it)
                a.dma_start(
                    out=yT.rearrange("(c p) t -> p c t", p=128)[:, :, j * C:(j + 1) * C],
                    in_=y_sb.rearrange("p (c t) -> p c t", t=C),
                ).then_inc(sem["s_y"], 16)
            em("act", _ydma)

        if do_ffn:
            for m in range(KH):
                s_id = slab_idx(it, "w12", m)
                ga1 = gidx(it, "a1", m)
                ga2 = gidx(it, "a2", m)
                def _mm1(pe, m=m, s_id=s_id, ga1=ga1, ga2=ga2, jb=jb, it=it):
                    if m == 0:
                        pe.wait_ge(sem["s_lnfr"], it)
                    pe.wait_ge(sem["s_wld"], 16 * (s_id + 1))
                    pa_wait(pe, ga1)
                    for k in range(KD):
                        ins = pe.matmul(pa_ap[ga1 % 4][:, 0:2 * C],
                                        wsl[s_id % NSL][:, k * 128:(k + 1) * 128],
                                        xnhl[jb][:, k * 2 * C:(k + 1) * 2 * C],
                                        start=(k == 0), stop=(k == KD - 1))
                    ins.then_inc(sem["s_pa"], 1)
                    pa_wait(pe, ga2)
                    for k in range(KD):
                        ins = pe.matmul(pa_ap[ga2 % 4][:, 0:2 * C],
                                        wsl[s_id % NSL][:, (KD + k) * 128:(KD + k + 1) * 128],
                                        xnhl[jb][:, k * 2 * C:(k + 1) * 2 * C],
                                        start=(k == 0), stop=(k == KD - 1))
                    ins.then_inc(sem["s_pa"], 1)
                em("pe", _mm1)

        if do_ln:
            def _stats(pe, i=i, ib=ib):
                if i == 0:
                    pe.wait_ge(sem["p_sm"], n_small)
                    pe.wait_ge(sem["s_a1"], 2)
                if i >= 1:
                    pe.wait_ge(sem["s_lnfr"], i)
                    pe.wait_ge(sem["s_projf"], 3 * (i - 1))
                pe.wait_ge(sem["s_xin"], 16 * (i + 1))
                for c in range(KD):
                    ins = pe.matmul(pc_ap[0:1, 0:C], ones_col,
                                    x_buf[ib][:, c * C:(c + 1) * C],
                                    start=(c == 0), stop=(c == KD - 1))
                ins.then_inc(sem["s_stat"], 1)
                pe.wait_ge(sem["s_sq"], i + 1)
                for c in range(KD):
                    ins = pe.matmul(pd_ap[0:1, 0:C], ones_col,
                                    sq[:, c * C:(c + 1) * C],
                                    start=(c == 0), stop=(c == KD - 1))
                ins.then_inc(sem["s_stat"], 1)
                pe.wait_ge(sem["s_muis"], i + 1)
                pe.wait_ge(sem["s_stfr"], 2 * i + 1)
                pe.matmul(pc_ap[:, 0:C], ones_row, musb,
                          start=True, stop=True).then_inc(sem["s_stat"], 1)
                pe.wait_ge(sem["s_stfr"], 2 * i + 2)
                pe.matmul(pd_ap[:, 0:C], ones_row, issb,
                          start=True, stop=True).then_inc(sem["s_stat"], 1)
            em("pe", _stats)

        if do_ffn:
            def _proj(pe, it=it):
                pe.wait_ge(sem["s_lnfr"], min(it + 1, nt))
                for gi, wmat in enumerate((win_d, win_b, win_c)):
                    if gi > 0:
                        pe.wait_ge(sem["s_projf"], 3 * (it - 1) + gi)
                    for c in range(KD):
                        if gi == 0:
                            pe.wait_ge(sem["s_xc"], KD * (it - 1) + c + 1)
                        ins = pe.matmul(pc_ap[0:N, 0:C],
                                        wmat[:, c * N:(c + 1) * N],
                                        xc[:, c * C:(c + 1) * C],
                                        start=(c == 0), stop=(c == KD - 1))
                    ins.then_inc(sem["s_proj"], 1)

            for m in range(KH):
                if m == 8:
                    em("pe", _proj)
                s_id = slab_idx(it, "h1", m)
                g = gidx(it, "mm2", m)
                def _mm2(pe, m=m, s_id=s_id, g=g, it=it):
                    if m == 0:
                        pe.wait_ge(sem["s_ghl"], KH * it)
                    pe.wait_ge(sem["s_wld"], 16 * (s_id + 1))
                    pa_wait(pe, g)
                    for k in range(KH):
                        ins = pe.matmul(pa_ap[g % 4][:, 0:2 * C],
                                        wsl[s_id % NSL][:, k * 128:(k + 1) * 128],
                                        ghl[:, k * 2 * C:(k + 1) * 2 * C],
                                        start=(k == 0), stop=(k == KH - 1))
                    ins.then_inc(sem["s_pa"], 1)
                em("pe", _mm2)
            for m in range(KH):
                s_id = slab_idx(it, "h2", m)
                g = gidx(it, "mm3", m)
                def _mm3(pe, m=m, s_id=s_id, g=g, it=it):
                    if m == 0:
                        pe.wait_ge(sem["s_bhl"], KH * it)
                    pe.wait_ge(sem["s_wld"], 16 * (s_id + 1))
                    pa_wait(pe, g)
                    for k in range(KH):
                        ins = pe.matmul(pa_ap[g % 4][:, 0:2 * C],
                                        wsl[s_id % NSL][:, k * 128:(k + 1) * 128],
                                        bhl[:, k * 2 * C:(k + 1) * 2 * C],
                                        start=(k == 0), stop=(k == KH - 1))
                    ins.then_inc(sem["s_pa"], 1)
                em("pe", _mm3)
            for m in range(KD):
                s_id = slab_idx(it, "w3", m)
                g = gidx(it, "mm4", m)
                def _mm4(pe, m=m, s_id=s_id, g=g, it=it):
                    if m == 0:
                        pe.wait_ge(sem["s_chl"], KH * it)
                        pe.wait_ge(sem["s_ssm"], it)
                    pe.wait_ge(sem["s_wld"], 16 * (s_id + 1))
                    pa_wait(pe, g)
                    for k in range(KH):
                        ins = pe.matmul(pa_ap[g % 4][:, 0:2 * C],
                                        wsl[s_id % NSL][:, k * 128:(k + 1) * 128],
                                        ghl[:, k * 2 * C:(k + 1) * 2 * C],
                                        start=(k == 0), stop=(k == KH - 1))
                    ins.then_inc(sem["s_pa"], 1)
                    pbu = 4 * (it - 1) + m // 2
                    if pbu >= 1:
                        pe.wait_ge(sem[f"s_pbf{m % 2}"], pbu)
                    pe.matmul(pb_ap[m % 2][:, 0:C],
                              wout_sb[:, m * 128:(m + 1) * 128],
                              chS, start=True, stop=True).then_inc(sem["s_pb"], 1)
                em("pe", _mm4)

        if do_ffn:
            for m in range(KH):
                ga1 = gidx(it, "a1", m)
                ga2 = gidx(it, "a2", m)
                def _gev(v, m=m, ga1=ga1, ga2=ga2, it=it):
                    if m >= 2:
                        v.wait_ge(sem["s_t1"], 96 * (it - 1) + m - 1)
                    v.wait_ge(sem["s_ha"], ga1 + 1)
                    v.tensor_add(asum[m % 2], pa_ap[ga1 % 4][:, 0:C],
                                 chalf[ga1 % 2]).then_inc(sem[f"s_paf{ga1 % 4}"], 1)
                    v.wait_ge(sem["s_ha"], ga2 + 1)
                    v.tensor_add(t2b[m % 2], pa_ap[ga2 % 4][:, 0:C],
                                 chalf[ga2 % 2]).then_inc(sem[f"s_paf{ga2 % 4}"], 1)
                    v.wait_ge(sem["s_t1"], 96 * (it - 1) + m + 1)
                    v.scalar_tensor_tensor(tmpc, t1b[m % 2], s2_sb[:, m:m + 1],
                                           t2b[m % 2], OP.mult, OP.mult)
                    hi = ghl[:, m * 2 * C:m * 2 * C + C]
                    lo = ghl[:, m * 2 * C + C:(m + 1) * 2 * C]
                    v.tensor_copy(hi, tmpc)
                    v.tensor_tensor(t2b[m % 2], tmpc, hi, OP.subtract)
                    if m == KH - 1:
                        v.tensor_copy(lo, t2b[m % 2])
                        v.drain().then_inc(sem["s_ghl"], 1)
                    else:
                        v.tensor_copy(lo, t2b[m % 2]).then_inc(sem["s_ghl"], 1)
                em("dve", _gev)

        if do_ln:
            def _var(v, i=i):
                v.wait_ge(sem["s_stfr"], 2 * i + 2)
                v.tensor_mul(varsb, musb, musb)
                v.tensor_sub(varsb, e2sb, varsb)
                v.tensor_scalar_add(varsb, varsb, EPS_LN)
                v.reciprocal(varsb, varsb).then_inc(sem["s_var"], 1)
            em("dve", _var)
            def _xn(v, i=i, ib=ib):
                v.wait_ge(sem["s_stat"], 4 * i + 4)
                if i >= 2:
                    v.wait_ge(sem["s_pa"], gidx(i - 1, "a2", KH - 1) + 1)
                for c in range(KD):
                    xch = x_buf[ib][:, c * C:(c + 1) * C]
                    v.tensor_sub(tmpc, xch, pc_ap[:, 0:C])
                    v.tensor_mul(tmpc, tmpc, pd_ap[:, 0:C])
                    hi = xnhl[ib][:, c * 2 * C:c * 2 * C + C]
                    lo = xnhl[ib][:, c * 2 * C + C:(c + 1) * 2 * C]
                    v.tensor_copy(hi, tmpc)
                    v.tensor_copy(stash[i % 3][:, c:c + 1], tmpc[:, C - 1:C])
                    v.tensor_tensor(tmpc, tmpc, hi, OP.subtract)
                    v.tensor_copy(lo, tmpc)
                v.drain().then_inc(sem["s_lnfr"], 1)
            em("dve", _xn)

        if do_ffn:
            def _conv(v, it=it, j=j, jb=jb, ib=ib,
                      first=first_of_seq, last=last_of_seq):
                v.wait_ge(sem["s_proj"], 3 * (it - 1))
                for c in range(KD):
                    hi = xnhl[jb][:, c * 2 * C:c * 2 * C + C]
                    lo = xnhl[jb][:, c * 2 * C + C:(c + 1) * 2 * C]
                    if first:
                        v.memset(halo[:, 0:1], 0.0)
                    else:
                        v.tensor_copy(halo[:, 0:1], stash[(j - 1) % 3][:, c:c + 1])
                    if last:
                        v.memset(halo[:, 1 + C:3 + C], 0.0)
                    else:
                        nhl = xnhl[ib]
                        v.tensor_tensor(halo[:, 1 + C:3 + C],
                                        nhl[:, c * 2 * C:c * 2 * C + 2],
                                        nhl[:, c * 2 * C + C:c * 2 * C + C + 2],
                                        OP.add)
                    v.tensor_tensor(halo[:, 1:1 + C], hi, lo, OP.add)
                    xcc = xc[:, c * C:(c + 1) * C]
                    v.tensor_scalar_mul(xcc, halo[:, 0:C], cw_sb[:, c * 4:c * 4 + 1])
                    for k in range(1, 4):
                        v.scalar_tensor_tensor(xcc, halo[:, k:k + C],
                                               cw_sb[:, c * 4 + k:c * 4 + k + 1],
                                               xcc, OP.mult, OP.add)
                    if last:
                        v.memset(xc[:, (c + 1) * C - 1:(c + 1) * C], 0.0)
                        v.tensor_scalar_add(xc[:, c * C:(c + 1) * C - 1],
                                            xc[:, c * C:(c + 1) * C - 1],
                                            cb_sb[:, c:c + 1])
                    else:
                        v.tensor_scalar_add(xcc, xcc, cb_sb[:, c:c + 1])
                    v.drain().then_inc(sem["s_xc"], 1)
            em("dve", _conv)

            def _ssm1(v, it=it):
                v.wait_ge(sem["s_proj"], 3 * (it - 1) + 1)
                v.tensor_scalar_add(d32, pc_ap[0:N, 0:C],
                                    b_d).then_inc(sem["s_projf"], 1)
                v.tensor_scalar_mul(sp1, d32, -1.0)
                v.tensor_tensor(sp1, sp1, d32, OP.max).then_inc(sem["s_d1"], 1)
                v.tensor_relu(sp3, d32)
                v.wait_ge(sem["s_proj"], 3 * (it - 1) + 2)
                v.tensor_scalar_add(b32, pc_ap[0:N, 0:C],
                                    b_b).then_inc(sem["s_projf"], 1)
                v.wait_ge(sem["s_proj"], 3 * (it - 1) + 3)
                v.tensor_scalar_add(c32, pc_ap[0:N, 0:C],
                                    b_c).then_inc(sem["s_projf"], 1)
                v.wait_ge(sem["s_a1"], 2 * it + 1)
                v.tensor_add(sp3, sp3, sp2)
                v.tensor_scalar_mul(dA, sp3, a_sb).then_inc(sem["s_d1"], 1)
            def _ssm2(v, it=it, first=first_of_seq):
                v.tensor_mul(tay, dA, dA)
                v.tensor_scalar(sp3, dA, 0.5, 1.0, OP.mult, OP.add)
                v.scalar_tensor_tensor(tay, tay, 1.0 / 6.0, sp3, OP.mult, OP.add)
                v.wait_ge(sem["s_a1"], 2 * it + 2)
                v.tensor_scalar_add(sp3, abar, -1.0)
                v.tensor_scalar_mul(sp3, sp3, inva_sb)
                v.tensor_sub(tay, tay, sp3)
                v.scalar_tensor_tensor(sp3, tay, mask_sb, sp3, OP.mult, OP.add)
                v.tensor_mul(bbar, sp3, b32)
                if first:
                    v.memset(carry, 0.0)
                    v.drain()
                v.tensor_tensor_scan(hS, abar, bbar, carry, OP.mult, OP.add)
                v.drain()
                v.tensor_copy(carry, hS[:, C - 1:C])
                v.drain()
                if it >= 2:
                    v.wait_ge(sem["s_pb"], KD * (it - 1))
                v.tensor_mul(chS, c32, hS)
                v.drain().then_inc(sem["s_ssm"], 1)

            for m in range(KH):
                if m == 4:
                    em("dve", _ssm1)
                g = gidx(it, "mm2", m)
                def _bspl(v, m=m, g=g, it=it):
                    if m >= 2:
                        v.wait_ge(sem["s_t1"], 96 * (it - 1) + 32 + m - 1)
                    v.wait_ge(sem["s_ha"], g + 1)
                    v.tensor_add(bsum[m % 2], pa_ap[g % 4][:, 0:C],
                                 chalf[g % 2]).then_inc(sem[f"s_paf{g % 4}"], 1)
                    v.wait_ge(sem["s_t1"], 96 * (it - 1) + 32 + m + 1)
                    hi = bhl[:, m * 2 * C:m * 2 * C + C]
                    lo = bhl[:, m * 2 * C + C:(m + 1) * 2 * C]
                    v.tensor_copy(hi, btb[m % 2])
                    v.tensor_tensor(tmpc, btb[m % 2], hi, OP.subtract)
                    if m == KH - 1:
                        v.tensor_copy(lo, tmpc)
                        v.drain().then_inc(sem["s_bhl"], 1)
                    else:
                        v.tensor_copy(lo, tmpc).then_inc(sem["s_bhl"], 1)
                em("dve", _bspl)
            em("dve", _ssm2)
            for m in range(KH):
                g = gidx(it, "mm3", m)
                def _cspl(v, m=m, g=g, it=it):
                    u = 32 + m
                    v.wait_ge(sem["s_t1"], 96 * (it - 1) + 32 + u - 1)
                    v.wait_ge(sem["s_ha"], g + 1)
                    v.tensor_add(bsum[m % 2], pa_ap[g % 4][:, 0:C],
                                 chalf[g % 2]).then_inc(sem[f"s_paf{g % 4}"], 1)
                    v.wait_ge(sem["s_t1"], 96 * (it - 1) + 64 + m + 1)
                    hi = ghl[:, m * 2 * C:m * 2 * C + C]
                    lo = ghl[:, m * 2 * C + C:(m + 1) * 2 * C]
                    v.tensor_copy(hi, btb[m % 2])
                    v.tensor_tensor(tmpc, btb[m % 2], hi, OP.subtract)
                    if m == KH - 1:
                        v.tensor_copy(lo, tmpc)
                        v.drain().then_inc(sem["s_chl"], 1)
                    else:
                        v.tensor_copy(lo, tmpc).then_inc(sem["s_chl"], 1)
                em("dve", _cspl)
            for m in range(KD):
                g = gidx(it, "mm4", m)
                def _out(v, m=m, g=g, it=it, jb=jb):
                    if m == 0 and it >= 2:
                        v.wait_ge(sem["s_y"], 16 * (it - 1))
                    v.wait_ge(sem["s_ha"], g + 1)
                    ych = y_sb[:, m * C:(m + 1) * C]
                    v.tensor_add(ych, pa_ap[g % 4][:, 0:C], chalf[g % 2]
                                 ).then_inc(sem[f"s_paf{g % 4}"], 1)
                    v.tensor_scalar_mul(ych, ych, s3_sb[:, m:m + 1])
                    v.wait_ge(sem["s_pb"], KD * (it - 1) + m + 1)
                    v.tensor_add(ych, ych, pb_ap[m % 2][:, 0:C]
                                 ).then_inc(sem[f"s_pbf{m % 2}"], 1)
                    v.tensor_add(ych, ych, x_buf[jb][:, m * C:(m + 1) * C])
                    ins = v.tensor_scalar_add(ych, ych, bout_sb[:, m:m + 1])
                    if m == KD - 1:
                        ins.then_inc(sem["s_out"], 1)
                em("dve", _out)

    em("act", lambda a: a.wait_ge(sem["s_y"], 16 * nt))
    em("sy", lambda q: q.wait_ge(sem["s_y"], 16 * nt))

    dbg_tensors = {}
    if debug and nt == 0:
        dbg_tensors = dict(
            win_dd=win_d, win_bb=win_b, win_cc=win_c, a_s=a_sb,
            mask_s=mask_sb, inva_s=inva_sb, bd_s=b_d, bb_s=b_b, bc_s=b_c,
            cw_s=cw_sb, cb_s=cb_sb, s1_s=s1_sb, s3_s=s3_sb, wout_s=wout_sb,
            bout_s=bout_sb, tb1_s=tb1, tb2_s=tb2, tn1_s=tn1, tn2_s=tn2,
            csl0=csl[0], cslq0=cslq[0],
        )
    elif debug:
        dbg_tensors = dict(
            x0=x_buf[0], x1=x_buf[1], xnhl0=xnhl[0], xnhl1=xnhl[1], sq=sq,
            xc=xc, d32=d32, b32=b32, c32=c32, sp1=sp1, sp2=sp2,
            sp3=sp3, dA=dA, abar=abar, tay=tay, bbar=bbar, hS=hS, chS=chS,
            carry=carry, ghl=ghl, bhl=bhl, y_sbuf=y_sb, t1b0=t1b[0],
            t2b0=t2b[0], btb0=btb[0], musb=musb, e2sb=e2sb, varsb=varsb,
            issb=issb, stash0=stash[0], stash1=stash[1], stash2=stash[2],
            win_dd=win_d, win_bb=win_b, win_cc=win_c, a_s=a_sb,
            mask_s=mask_sb, inva_s=inva_sb, bd_s=b_d, bb_s=b_b, bc_s=b_c,
            cw_s=cw_sb, cb_s=cb_sb, s1_s=s1_sb, s3_s=s3_sb, wsl0=wsl[0],
            wsl1=wsl[1], wsl2=wsl[2], wsl3=wsl[3], wout_s=wout_sb,
            bout_s=bout_sb, w12q0=w12q[0], w12q7=w12q[7], h1q0=h1q[0],
            h1q31=h1q[31], h2q0=h2q[0], w3q0=w3q[0],
        )

    dbg_out = {}
    if debug:
        for k, ap in dbg_tensors.items():
            sh = [ap.shape[0], int(np.prod(list(ap.shape)[1:]))]
            dbg_out[k] = nc.dram_tensor("dbg_" + k, sh, ap.tensor.dtype,
                                        kind="ExternalOutput").ap()

    with nc.Block() as blk:
        @blk.sync
        def _(q):
            for fn in pro["sy"] + ops["sy"]:
                fn(q)
        @blk.gpsimd
        def _(g):
            for fn in pro["gp"] + ops["gp"]:
                fn(g)
        @blk.tensor
        def _(pe):
            for fn in pro["pe"] + ops["pe"]:
                fn(pe)
        @blk.vector
        def _(v):
            for fn in pro["dve"] + ops["dve"]:
                fn(v)
        @blk.scalar
        def _(a):
            for fn in pro["act"] + ops["act"]:
                fn(a)

    if debug:
        dsem = nc.alloc_semaphore(name="dsem")
        with nc.Block() as blk:
            @blk.sync
            def _(q):
                n = 0
                for k, ap in dbg_tensors.items():
                    q.dma_start(out=dbg_out[k], in_=ap).then_inc(dsem, 16)
                    n += 1
                q.wait_ge(dsem, 16 * n)

    return nc


def _host_prep(inputs):
    f = np.float32
    att_x = np.asarray(inputs["L0_attn_x"], dtype=f)
    w1T = np.ascontiguousarray(np.asarray(inputs["L0_ffn_w1_w"], dtype=f).T)
    w2T = np.ascontiguousarray(np.asarray(inputs["L0_ffn_w2_w"], dtype=f).T)
    h1T = np.ascontiguousarray(np.asarray(inputs["L0_ffn_h1_w"], dtype=f).T)
    h2T = np.ascontiguousarray(np.asarray(inputs["L0_ffn_h2_w"], dtype=f).T)
    w3T = np.ascontiguousarray(np.asarray(inputs["L0_ffn_w3_w"], dtype=f).T)

    def blk(wT, km, mm):
        return np.ascontiguousarray(
            wT.reshape(km, 128, mm, 128).transpose(2, 1, 0, 3).reshape(mm, 128, km * 128))

    w1b = blk(w1T, KD, KH)
    w2b = blk(w2T, KD, KH)
    w12s = np.ascontiguousarray(np.concatenate([w1b, w2b], axis=2))
    h1s = blk(h1T, KH, KH)
    h2s = blk(h2T, KH, KH)
    w3s = blk(w3T, KH, KD)

    win = np.asarray(inputs["L0_attn_Win"], dtype=f)
    def winblk(w):
        return np.ascontiguousarray(w.reshape(KD, 128, N).transpose(1, 0, 2).reshape(128, KD * N))
    wind_h = winblk(win[:, :N])
    winbe_h = winblk(win[:, N:3 * N:2])
    winbo_h = winblk(win[:, N + 1:3 * N:2])
    wince_h = winblk(win[:, 3 * N:5 * N:2])
    winco_h = winblk(win[:, 3 * N + 1:5 * N:2])
    b_in = np.asarray(inputs["L0_attn_b_in"], dtype=f)
    bd_h = np.ascontiguousarray(b_in[:N].reshape(N, 1))
    bbe_h = np.ascontiguousarray(b_in[N:3 * N:2].reshape(N, 1))
    bbo_h = np.ascontiguousarray(b_in[N + 1:3 * N:2].reshape(N, 1))
    bce_h = np.ascontiguousarray(b_in[3 * N:5 * N:2].reshape(N, 1))
    bco_h = np.ascontiguousarray(b_in[3 * N + 1:5 * N:2].reshape(N, 1))
    wout_h = np.ascontiguousarray(np.asarray(inputs["L0_attn_Wout"], dtype=f))
    bout_h = np.ascontiguousarray(np.asarray(inputs["L0_attn_b_out"], dtype=f).reshape(KD, 128).T)
    alog_h = np.ascontiguousarray(np.asarray(inputs["L0_attn_A_log"], dtype=f).reshape(N, 1))
    cw = np.asarray(inputs["L0_attn_conv_w"], dtype=f).reshape(D, 4)
    cw_h = np.ascontiguousarray(cw.reshape(KD, 128, 4).transpose(1, 0, 2).reshape(128, KD * 4))
    cb_h = np.ascontiguousarray(np.asarray(inputs["L0_attn_conv_b"], dtype=f).reshape(KD, 128).T)

    def sc(v, km):
        return np.ascontiguousarray(np.asarray(v, dtype=f).reshape(km, 128).T)

    shared = dict(
        w12s=w12s, h1s=h1s, h2s=h2s, w3s=w3s, wind_h=wind_h, winbe_h=winbe_h,
        winbo_h=winbo_h, wince_h=wince_h, winco_h=winco_h, bd_h=bd_h,
        bbe_h=bbe_h, bbo_h=bbo_h, bce_h=bce_h, bco_h=bco_h,
        wout_h=wout_h, bout_h=bout_h, alog_h=alog_h, cw_h=cw_h, cb_h=cb_h,
        s1_h=sc(inputs["L0_ffn_w1_s"], KH), s2_h=sc(inputs["L0_ffn_w2_s"], KH),
        sh1_h=sc(inputs["L0_ffn_h1_s"], KH), sh2_h=sc(inputs["L0_ffn_h2_s"], KH),
        s3_h=sc(inputs["L0_ffn_w3_s"], KD),
    )
    in_maps = []
    for c in range(NCORE):
        pair = att_x[SEQ_PER_CORE * c:SEQ_PER_CORE * (c + 1)]
        xTc = np.ascontiguousarray(pair.transpose(2, 0, 1).reshape(D, SEQ_PER_CORE * L))
        m = dict(shared)
        m["xT"] = xTc
        in_maps.append(m)
    return in_maps


def kernel(**inputs):
    in_maps = _host_prep(inputs)
    nc = build_program(NT)
    res = run_bass_kernel_spmd(nc, in_maps, core_ids=list(range(NCORE)))
    out = np.empty((B, L, D), dtype=np.float32)
    for c in range(NCORE):
        yT = res.results[c]["yT"]
        pair = yT.reshape(D, SEQ_PER_CORE, L).transpose(1, 2, 0)
        out[SEQ_PER_CORE * c:SEQ_PER_CORE * (c + 1)] = pair
    return out


# version 09 (final)
# speedup vs baseline: 1.0715x; 1.0633x over previous
import numpy as np
import concourse.bass as bass
import concourse.mybir as mybir
from concourse.bass_utils import run_bass_kernel_spmd

F32 = mybir.dt.float32
F16 = mybir.dt.float16
FP8 = mybir.dt.float8e4
AF = mybir.ActivationFunctionType
OP = mybir.AluOpType

D = 1024
H = 4096
L = 2048
B = 16
NCORE = 8
SEQ_PER_CORE = 2
C = 256
KD = D // 128
KH = H // 128
TILES_PER_SEQ = L // C
NT = SEQ_PER_CORE * TILES_PER_SEQ
N = 16
EPS_LN = 1e-5
EPS_A = 1e-4
NSL = 6


def build_program(nt=NT, debug=False):
    nc = bass.Bass("TRN2", target_bir_lowering=False, detect_race_conditions=False)

    xT = nc.dram_tensor("xT", [D, SEQ_PER_CORE * L], F32, kind="ExternalInput").ap()
    w12s = nc.dram_tensor("w12s", [KH, 128, 2 * KD * 128], F32, kind="ExternalInput").ap()
    h1s = nc.dram_tensor("h1s", [KH, 128, KH * 128], F32, kind="ExternalInput").ap()
    h2s = nc.dram_tensor("h2s", [KH, 128, KH * 128], F32, kind="ExternalInput").ap()
    w3s = nc.dram_tensor("w3s", [KD, 128, KH * 128], F32, kind="ExternalInput").ap()
    wind_h = nc.dram_tensor("wind_h", [128, KD * N], F32, kind="ExternalInput").ap()
    winbe_h = nc.dram_tensor("winbe_h", [128, KD * N], F32, kind="ExternalInput").ap()
    winbo_h = nc.dram_tensor("winbo_h", [128, KD * N], F32, kind="ExternalInput").ap()
    wince_h = nc.dram_tensor("wince_h", [128, KD * N], F32, kind="ExternalInput").ap()
    winco_h = nc.dram_tensor("winco_h", [128, KD * N], F32, kind="ExternalInput").ap()
    bd_h = nc.dram_tensor("bd_h", [N, 1], F32, kind="ExternalInput").ap()
    bbe_h = nc.dram_tensor("bbe_h", [N, 1], F32, kind="ExternalInput").ap()
    bbo_h = nc.dram_tensor("bbo_h", [N, 1], F32, kind="ExternalInput").ap()
    bce_h = nc.dram_tensor("bce_h", [N, 1], F32, kind="ExternalInput").ap()
    bco_h = nc.dram_tensor("bco_h", [N, 1], F32, kind="ExternalInput").ap()
    wout_h = nc.dram_tensor("wout_h", [N, D], F32, kind="ExternalInput").ap()
    bout_h = nc.dram_tensor("bout_h", [128, KD], F32, kind="ExternalInput").ap()
    alog_h = nc.dram_tensor("alog_h", [N, 1], F32, kind="ExternalInput").ap()
    cw_h = nc.dram_tensor("cw_h", [128, KD * 4], F32, kind="ExternalInput").ap()
    cb_h = nc.dram_tensor("cb_h", [128, KD], F32, kind="ExternalInput").ap()
    s1_h = nc.dram_tensor("s1_h", [128, KH], F32, kind="ExternalInput").ap()
    s2_h = nc.dram_tensor("s2_h", [128, KH], F32, kind="ExternalInput").ap()
    sh1_h = nc.dram_tensor("sh1_h", [128, KH], F32, kind="ExternalInput").ap()
    sh2_h = nc.dram_tensor("sh2_h", [128, KH], F32, kind="ExternalInput").ap()
    s3_h = nc.dram_tensor("s3_h", [128, KD], F32, kind="ExternalInput").ap()
    yT = nc.dram_tensor("yT", [D, SEQ_PER_CORE * L], F32, kind="ExternalOutput").ap()
    w12q = nc.dram_tensor("w12q", [KH, 128, 2 * KD * 128], FP8, kind="Internal").ap()
    h1q = nc.dram_tensor("h1q", [KH, 128, KH * 128], FP8, kind="Internal").ap()
    h2q = nc.dram_tensor("h2q", [KH, 128, KH * 128], FP8, kind="Internal").ap()
    w3q = nc.dram_tensor("w3q", [KD, 128, KH * 128], FP8, kind="Internal").ap()

    S = nc.alloc_sbuf_tensor
    x_buf = [S(f"x{i}", [128, KD * C], F32).ap() for i in range(2)]
    xnhl = [S(f"xnhl{i}", [128, KD * 2 * C], F16).ap() for i in range(2)]
    ghl_h = S("ghl", [128, KH * 2 * C], F16)
    bhl_h = S("bhl", [128, KH * 2 * C], F16)
    ghl = ghl_h.ap()
    bhl = bhl_h.ap()
    wsl_h = [S(f"wsl{i}", [128, KH * 128], FP8) for i in range(NSL)]
    wsl = [t.ap() for t in wsl_h]
    y_sb = S("y_sb", [128, KD * C], F32).ap()
    sq = S("sq", [128, KD * C], F32).ap()
    xc = S("xc", [128, KD * C], F32).ap()
    halo = S("halo", [128, C + 4], F32).ap()
    t1b = [S(f"t1b{i}", [128, C], F32).ap() for i in range(2)]
    chalf = [S(f"chalf{i}", [128, C], F32).ap() for i in range(2)]
    asum = [S(f"asum{i}", [128, C], F32).ap() for i in range(2)]
    bsum = [S(f"bsum{i}", [128, C], F32).ap() for i in range(2)]
    t2b = [S(f"t2b{i}", [128, C], F32).ap() for i in range(2)]
    btb = [S(f"btb{i}", [128, C], F32).ap() for i in range(2)]
    tmpc = S("tmpc", [128, C], F32).ap()
    tmpd = S("tmpd", [128, C], F32).ap()
    stash = [S(f"stash{i}", [128, KD], F32).ap() for i in range(3)]
    musb = S("musb", [1, C], F32).ap()
    e2sb = S("e2sb", [1, C], F32).ap()
    varsb = S("varsb", [1, C], F32).ap()
    issb = S("issb", [1, C], F32).ap()
    ones_col = S("ones_col", [128, 1], F32).ap()
    ones_row = S("ones_row", [1, 128], F32).ap()
    win_d = S("win_d", [128, KD * N], F32).ap()
    win_b = S("win_b", [128, KD * N], F32).ap()
    win_c = S("win_c", [128, KD * N], F32).ap()
    win_t = S("win_t", [128, KD * N], F32).ap()
    win_t2 = S("win_t2", [128, KD * N], F32).ap()
    wout_sb = S("wout_sb", [N, D], F32).ap()
    b_d = S("b_d", [N, 1], F32).ap()
    b_b = S("b_b", [N, 1], F32).ap()
    b_c = S("b_c", [N, 1], F32).ap()
    tb1 = S("tb1", [N, 1], F32).ap()
    tb2 = S("tb2", [N, 1], F32).ap()
    dum = S("dum", [N, 1], F32).ap()
    bout_sb = S("bout_sb", [128, KD], F32).ap()
    cw_sb = S("cw_sb", [128, KD * 4], F32).ap()
    cb_sb = S("cb_sb", [128, KD], F32).ap()
    s1_sb = S("s1_sb", [128, KH], F32).ap()
    s2_sb = S("s2_sb", [128, KH], F32).ap()
    sh1_sb = S("sh1_sb", [128, KH], F32).ap()
    sh2_sb = S("sh2_sb", [128, KH], F32).ap()
    s3_sb = S("s3_sb", [128, KD], F32).ap()
    alog_sb = S("alog_sb", [N, 1], F32).ap()
    a_sb = S("a_sb", [N, 1], F32).ap()
    mask_sb = S("mask_sb", [N, 1], F32).ap()
    inva_sb = S("inva_sb", [N, 1], F32).ap()
    tn1 = S("tn1", [N, 1], F32).ap()
    tn2 = S("tn2", [N, 1], F32).ap()
    d32 = S("d32", [N, C], F32).ap()
    b32 = S("b32", [N, C], F32).ap()
    c32 = S("c32", [N, C], F32).ap()
    sp1 = S("sp1", [N, C], F32).ap()
    sp2 = S("sp2", [N, C], F32).ap()
    sp3 = S("sp3", [N, C], F32).ap()
    dA = S("dA", [N, C], F32).ap()
    abar = S("abar", [N, C], F32).ap()
    tay = S("tay", [N, C], F32).ap()
    bbar = S("bbar", [N, C], F32).ap()
    hS = S("hS", [N, C], F32).ap()
    chS = S("chS", [N, C], F32).ap()
    carry = S("carry", [N, 1], F32).ap()
    CW = 1024
    NCB = 3
    csl = [S(f"csl{i}", [128, CW], F32).ap() for i in range(NCB)]
    cslq = [S(f"cslq{i}", [128, CW], FP8).ap() for i in range(NCB)]

    P = nc.alloc_psum_tensor
    pa = [P(f"pa{i}", [128, 512], F32) for i in range(4)]
    pb = [P(f"pb{i}", [128, 512], F32) for i in range(2)]
    pc_t = P("pc", [128, 512], F32)
    pd_t = P("pd", [128, 512], F32)
    pa_ap = [t.ap() for t in pa]
    pb_ap = [t.ap() for t in pb]
    pc_ap = pc_t.ap()
    pd_ap = pd_t.ap()

    sem_names = ["s_wld", "s_xin", "s_y", "s_out", "s_sq", "s_stat", "s_stfr",
                 "s_var", "s_muis", "s_lnfr", "s_xc", "s_proj", "s_projf",
                 "s_ssm", "s_pa", "s_paf0", "s_paf1", "s_paf2", "s_paf3",
                 "s_pb", "s_pbf0", "s_pbf1", "s_ghl", "s_bhl", "s_chl",
                 "s_d1", "s_a1", "s_ha", "s_t1", "c_ld", "c_done_d", "c_done_g", "c_st", "p_sm"]
    sem = {n: nc.alloc_semaphore(name=n) for n in sem_names}

    ops = {"pe": [], "act": [], "dve": [], "gp": [], "sy": []}
    pro = {"pe": [], "act": [], "dve": [], "gp": [], "sy": []}

    def em(engine, fn):
        ops[engine].append(fn)

    def pem(engine, fn):
        pro[engine].append(fn)

    smalls = [(win_d, wind_h), (win_b, winbe_h), (win_c, wince_h),
              (b_d, bd_h), (tb1, bbe_h), (tb2, bbo_h),
              (tn1, bce_h), (tn2, bco_h), (wout_sb, wout_h),
              (bout_sb, bout_h), (alog_sb, alog_h), (cw_sb, cw_h),
              (cb_sb, cb_h), (s1_sb, s1_h), (s2_sb, s2_h), (sh1_sb, sh1_h),
              (sh2_sb, sh2_h), (s3_sb, s3_h)]
    for dst, src in smalls:
        pem("sy", lambda q, d=dst, s=src: q.dma_start(out=d, in_=s).then_inc(sem["p_sm"], 16))
    n_small = len(smalls) * 16

    def _win_odds(q):
        q.dma_start(out=win_t, in_=winbo_h).then_inc(sem["p_sm"], 16)
        q.dma_start(out=win_t2, in_=winco_h).then_inc(sem["p_sm"], 16)
        q.dma_start(out=dum, in_=bd_h).then_inc(sem["p_sm"], 16)
    pem("sy", _win_odds)


    def _prep_dve(v):
        v.wait_ge(sem["p_sm"], n_small + 48)
        v.tensor_add(b_b, tb1, tb2)
        v.drain()
        v.tensor_scalar_mul(b_b, b_b, 0.5)
        v.tensor_add(b_c, tn1, tn2)
        v.drain()
        v.tensor_scalar_mul(b_c, b_c, 0.5)
        v.drain()
        v.memset(ones_col, 1.0)
        v.memset(ones_row, 1.0)
        v.memset(carry, 0.0)
        v.memset(halo, 0.0)
        for st_t in stash:
            v.memset(st_t, 0.0)
        v.wait_ge(sem["p_sm"], n_small + 48)
        v.tensor_add(win_b, win_b, win_t)
        v.tensor_scalar_mul(win_b, win_b, 0.5)
        v.tensor_add(win_c, win_c, win_t2)
        v.tensor_scalar_mul(win_c, win_c, 0.5)
        v.drain().then_inc(sem["s_a1"], 1)
    pem("dve", _prep_dve)

    def _prep_act(a):
        a.wait_ge(sem["s_a1"], 1)
        a.activation(out=a_sb, in_=alog_sb, func=AF.Exp)
        a.activation(out=a_sb, in_=a_sb, func=AF.Copy, scale=-1.0)
        a.sign(tn1, a_sb)
        a.activation(out=tn1, in_=tn1, func=AF.Copy, scale=EPS_A,
                     bias=EPS_A).then_inc(sem["s_d1"], 1)
    pem("act", _prep_act)

    def _prep_dve2(v):
        v.wait_ge(sem["s_d1"], 1)
        v.tensor_scalar_mul(tn2, a_sb, -1.0)
        v.drain()
        v.tensor_tensor(tn2, tn2, a_sb, OP.max)
        v.drain()
        v.tensor_scalar(mask_sb, tn2, -1e30, EPS_A * 1e30, OP.mult, OP.add)
        v.drain()
        v.tensor_relu(mask_sb, mask_sb)
        v.drain()
        v.tensor_scalar_min(mask_sb, mask_sb, 1.0)
        v.tensor_sub(tn1, tn1, a_sb)
        v.drain()
        v.tensor_mul(tn1, tn1, mask_sb)
        v.drain()
        v.tensor_add(tn2, tn1, a_sb)
        v.drain()
        v.reciprocal(inva_sb, tn2)
        v.drain()
        v.engine_nop().then_inc(sem["s_a1"], 1)
    pem("dve", _prep_dve2)

    cast_units = []
    for src3, dst3, mm, W in ((w12s, w12q, KH, 2 * KD * 128),
                              (h1s, h1q, KH, KH * 128),
                              (h2s, h2q, KH, KH * 128),
                              (w3s, w3q, KD, KH * 128)):
        for m in range(mm):
            for hf in range(W // CW):
                cast_units.append((src3[m][:, hf * CW:(hf + 1) * CW],
                                   dst3[m][:, hf * CW:(hf + 1) * CW]))
    NCS = len(cast_units)

    def _cast_sync(q):
        for u, (srcu, dstu) in enumerate(cast_units):
            if u >= NCB:
                q.wait_ge(sem["c_done_d"], u - NCB + 1)
            q.dma_start(out=csl[u % NCB], in_=srcu).then_inc(sem["c_ld"], 16)
            if u >= NCB:
                pu = u - NCB
                q.dma_start(out=cast_units[pu][1],
                            in_=cslq[pu % NCB]).then_inc(sem["c_st"], 16)
        q.dma_start(out=dum, in_=bd_h).then_inc(sem["c_ld"], 16)
        for pu in range(NCS - NCB, NCS):
            q.wait_ge(sem["c_done_d"], pu + 1)
            q.dma_start(out=cast_units[pu][1],
                        in_=cslq[pu % NCB]).then_inc(sem["c_st"], 16)
        q.wait_ge(sem["c_st"], 16 * NCS)
    pem("sy", _cast_sync)

    def _cast_dve(v):
        for u in range(NCS):
            v.wait_ge(sem["c_ld"], 16 * min(u + 2, NCS + 1))
            if u >= NCB:
                v.wait_ge(sem["c_st"], 16 * (u - NCB + 1))
            v.tensor_copy(cslq[u % NCB], csl[u % NCB])
            v.drain().then_inc(sem["c_done_d"], 1)
    pem("dve", _cast_dve)

    GP_PER_IT = 2 * KH + KH + KH + KD

    def gidx(it, phase, m):
        base = (it - 1) * GP_PER_IT
        if phase == "a1":
            return base + 2 * m
        if phase == "a2":
            return base + 2 * m + 1
        if phase == "mm2":
            return base + 2 * KH + m
        if phase == "mm3":
            return base + 3 * KH + m
        return base + 4 * KH + m

    SL_PER_IT = 3 * KH + KD

    def slab_idx(it, phase, m):
        base = (it - 1) * SL_PER_IT
        off = {"w12": 0, "h1": KH, "h2": 2 * KH, "w3": 3 * KH}[phase]
        return base + off + m

    slab_tab = []
    for it in range(1, nt + 1):
        for m in range(KH):
            slab_tab.append((w12q[m], 2 * KD * 128, gidx(it, "a2", m)))
        for m in range(KH):
            slab_tab.append((h1q[m], KH * 128, gidx(it, "mm2", m)))
        for m in range(KH):
            slab_tab.append((h2q[m], KH * 128, gidx(it, "mm3", m)))
        for m in range(KD):
            slab_tab.append((w3q[m], KH * 128, gidx(it, "mm4", m)))

    def cst_gate(s_i):
        m = s_i % KH
        if s_i < KH:
            return 2 * m + 2
        if s_i < 2 * KH:
            return 64 + 4 * m + 4
        if s_i < 3 * KH:
            return 192 + 4 * m + 4
        return 320 + 4 * (s_i - 3 * KH) + 4

    NSLAB = len(slab_tab)
    for s_i, (src, w, _lg) in enumerate(slab_tab):
        def _ld(q, src=src, w=w, s_i=s_i):
            if s_i >= NSL:
                q.wait_ge(sem["s_pa"], slab_tab[s_i - NSL][2] + 1)
            q.dma_start(out=wsl[s_i % NSL][:, 0:w], in_=src).then_inc(sem["s_wld"], 16)
        em("sy", _ld)
    em("sy", lambda q: q.dma_start(out=dum, in_=bd_h).then_inc(sem["s_wld"], 16))

    def pa_wait(pe, g):
        use = g // 4
        if use >= 1:
            pe.wait_ge(sem[f"s_paf{g % 4}"], use)

    n_iters = nt + 1
    for it in range(n_iters):
        i = it
        j = it - 1
        do_ln = i < nt
        do_ffn = j >= 0
        jb = j % 2
        ib = i % 2
        first_of_seq = do_ffn and (j % TILES_PER_SEQ) == 0
        last_of_seq = do_ffn and (j % TILES_PER_SEQ) == TILES_PER_SEQ - 1

        if do_ln:
            def _xdma(a, i=i, ib=ib):
                if i >= 2:
                    a.wait_ge(sem["s_out"], i - 1)
                a.dma_start(
                    out=x_buf[ib].rearrange("p (c t) -> p c t", t=C),
                    in_=xT.rearrange("(c p) t -> p c t", p=128)[:, :, i * C:(i + 1) * C],
                ).then_inc(sem["s_xin"], 16)
                a.dma_start(out=dum, in_=bd_h).then_inc(sem["s_xin"], 16)
            em("act", _xdma)

        if do_ffn:
            def cp_wait(a, g):
                if g >= 2:
                    a.wait_ge(sem[f"s_paf{(g - 2) % 4}"], (g - 2) // 4 + 1)
            for m in range(KH):
                ga1 = gidx(it, "a1", m)
                ga2 = gidx(it, "a2", m)
                def _silu(a, m=m, ga1=ga1, ga2=ga2, it=it):
                    cp_wait(a, ga1)
                    a.wait_ge(sem["s_pa"], ga1 + 1)
                    a.copy(chalf[ga1 % 2], pa_ap[ga1 % 4][:, C:2 * C]
                           ).then_inc(sem["s_ha"], 1)
                    if m >= 2:
                        a.wait_ge(sem["s_ghl"], (it - 1) * KH + m - 1)
                    a.wait_ge(sem[f"s_paf{ga1 % 4}"], ga1 // 4 + 1)
                    a.activation(out=t1b[m % 2], in_=asum[m % 2],
                                 func=AF.Silu, scale=s1_sb[:, m:m + 1]
                                 ).then_inc(sem["s_t1"], 1)
                    cp_wait(a, ga2)
                    a.wait_ge(sem["s_pa"], ga2 + 1)
                    a.copy(chalf[ga2 % 2], pa_ap[ga2 % 4][:, C:2 * C]
                           ).then_inc(sem["s_ha"], 1)
                em("act", _silu)

        if do_ln:
            def _sqs(a, i=i, ib=ib):
                a.wait_ge(sem["s_xin"], 32 * (i + 1))
                if i >= 1:
                    a.wait_ge(sem["s_stat"], 4 * (i - 1) + 2)
                for c in range(KD):
                    a.activation(out=sq[:, c * C:(c + 1) * C],
                                 in_=x_buf[ib][:, c * C:(c + 1) * C],
                                 func=AF.Square)
                a.drain().then_inc(sem["s_sq"], 1)
            em("act", _sqs)
            def _mu(a, i=i):
                a.wait_ge(sem["s_stat"], 4 * i + 1)
                a.activation(out=musb, in_=pc_ap[0:1, 0:C], func=AF.Copy,
                             scale=1.0 / D).then_inc(sem["s_stfr"], 1)
                a.wait_ge(sem["s_stat"], 4 * i + 2)
                a.activation(out=e2sb, in_=pd_ap[0:1, 0:C], func=AF.Copy,
                             scale=1.0 / D).then_inc(sem["s_stfr"], 1)
            em("act", _mu)
            def _rs(a, i=i):
                a.wait_ge(sem["s_var"], i + 1)
                a.activation(out=issb, in_=varsb,
                             func=AF.Sqrt).then_inc(sem["s_muis"], 1)
            em("act", _rs)

        if do_ffn:
            def _ssm_act1(a, it=it):
                a.wait_ge(sem["s_d1"], 2 * it)
                a.activation(out=sp2, in_=sp1, func=AF.Exp, scale=-1.0)
                a.activation(out=sp2, in_=sp2, func=AF.Ln,
                             bias=1.0).then_inc(sem["s_a1"], 1)
            def _ssm_act2(a, it=it):
                a.wait_ge(sem["s_d1"], 2 * it + 1)
                a.activation(out=abar, in_=dA, func=AF.Exp).then_inc(sem["s_a1"], 1)

            for m in range(KH):
                if m == 4:
                    em("act", _ssm_act1)
                g = gidx(it, "mm2", m)
                def _gelu(a, m=m, g=g, it=it):
                    cp_wait(a, g)
                    a.wait_ge(sem["s_pa"], g + 1)
                    a.copy(chalf[g % 2], pa_ap[g % 4][:, C:2 * C]
                           ).then_inc(sem["s_ha"], 1)
                    if m >= 2:
                        a.wait_ge(sem["s_bhl"], (it - 1) * KH + m - 1)
                    a.wait_ge(sem[f"s_paf{g % 4}"], g // 4 + 1)
                    a.activation(out=btb[m % 2], in_=bsum[m % 2],
                                 func=AF.Gelu, scale=sh1_sb[:, m:m + 1]
                                 ).then_inc(sem["s_t1"], 1)
                em("act", _gelu)
            em("act", _ssm_act2)
            for m in range(KH):
                g = gidx(it, "mm3", m)
                def _gelu3(a, m=m, g=g, it=it):
                    cp_wait(a, g)
                    a.wait_ge(sem["s_pa"], g + 1)
                    a.copy(chalf[g % 2], pa_ap[g % 4][:, C:2 * C]
                           ).then_inc(sem["s_ha"], 1)
                    if m == 0:
                        a.wait_ge(sem["s_bhl"], (it - 1) * KH + KH - 1)
                    elif m == 1:
                        a.wait_ge(sem["s_bhl"], (it - 1) * KH + KH)
                    else:
                        a.wait_ge(sem["s_chl"], (it - 1) * KH + m - 1)
                    a.wait_ge(sem[f"s_paf{g % 4}"], g // 4 + 1)
                    a.activation(out=btb[m % 2], in_=bsum[m % 2],
                                 func=AF.Gelu, scale=sh2_sb[:, m:m + 1]
                                 ).then_inc(sem["s_t1"], 1)
                em("act", _gelu3)

            for m in range(KD):
                g = gidx(it, "mm4", m)
                def _cp4(a, m=m, g=g):
                    cp_wait(a, g)
                    a.wait_ge(sem["s_pa"], g + 1)
                    a.copy(chalf[g % 2], pa_ap[g % 4][:, C:2 * C]
                           ).then_inc(sem["s_ha"], 1)
                em("act", _cp4)

            def _ydma(a, it=it, j=j):
                a.wait_ge(sem["s_out"], it)
                a.dma_start(
                    out=yT.rearrange("(c p) t -> p c t", p=128)[:, :, j * C:(j + 1) * C],
                    in_=y_sb.rearrange("p (c t) -> p c t", t=C),
                ).then_inc(sem["s_y"], 16)
            em("act", _ydma)

        if do_ffn:
            for m in range(KH):
                s_id = slab_idx(it, "w12", m)
                ga1 = gidx(it, "a1", m)
                ga2 = gidx(it, "a2", m)
                def _mm1(pe, m=m, s_id=s_id, ga1=ga1, ga2=ga2, jb=jb, it=it):
                    if m == 0:
                        pe.wait_ge(sem["s_lnfr"], it)
                    pe.wait_ge(sem["s_wld"], 16 * min(s_id + 2, NSLAB + 1))
                    pa_wait(pe, ga1)
                    for k in range(KD):
                        ins = pe.matmul(pa_ap[ga1 % 4][:, 0:2 * C],
                                        wsl[s_id % NSL][:, k * 128:(k + 1) * 128],
                                        xnhl[jb][:, k * 2 * C:(k + 1) * 2 * C],
                                        start=(k == 0), stop=(k == KD - 1))
                    ins.then_inc(sem["s_pa"], 1)
                    pa_wait(pe, ga2)
                    for k in range(KD):
                        ins = pe.matmul(pa_ap[ga2 % 4][:, 0:2 * C],
                                        wsl[s_id % NSL][:, (KD + k) * 128:(KD + k + 1) * 128],
                                        xnhl[jb][:, k * 2 * C:(k + 1) * 2 * C],
                                        start=(k == 0), stop=(k == KD - 1))
                    ins.then_inc(sem["s_pa"], 1)
                em("pe", _mm1)

        if do_ln:
            def _stats(pe, i=i, ib=ib):
                if i == 0:
                    pe.wait_ge(sem["p_sm"], n_small + 48)
                    pe.wait_ge(sem["s_a1"], 2)
                if i >= 1:
                    pe.wait_ge(sem["s_lnfr"], i)
                    pe.wait_ge(sem["s_projf"], 3 * (i - 1))
                pe.wait_ge(sem["s_xin"], 32 * (i + 1))
                for c in range(KD):
                    ins = pe.matmul(pc_ap[0:1, 0:C], ones_col,
                                    x_buf[ib][:, c * C:(c + 1) * C],
                                    start=(c == 0), stop=(c == KD - 1))
                ins.then_inc(sem["s_stat"], 1)
                pe.wait_ge(sem["s_sq"], i + 1)
                for c in range(KD):
                    ins = pe.matmul(pd_ap[0:1, 0:C], ones_col,
                                    sq[:, c * C:(c + 1) * C],
                                    start=(c == 0), stop=(c == KD - 1))
                ins.then_inc(sem["s_stat"], 1)
                pe.wait_ge(sem["s_muis"], i + 1)
                pe.wait_ge(sem["s_stfr"], 2 * i + 1)
                pe.matmul(pc_ap[:, 0:C], ones_row, musb,
                          start=True, stop=True).then_inc(sem["s_stat"], 1)
                pe.wait_ge(sem["s_stfr"], 2 * i + 2)
                pe.matmul(pd_ap[:, 0:C], ones_row, issb,
                          start=True, stop=True).then_inc(sem["s_stat"], 1)
            em("pe", _stats)

        if do_ffn:
            def _proj(pe, it=it):
                pe.wait_ge(sem["s_lnfr"], min(it + 1, nt))
                for gi, wmat in enumerate((win_d, win_b, win_c)):
                    if gi > 0:
                        pe.wait_ge(sem["s_projf"], 3 * (it - 1) + gi)
                    for c in range(KD):
                        if gi == 0:
                            pe.wait_ge(sem["s_xc"], KD * (it - 1) + c + 1)
                        ins = pe.matmul(pc_ap[0:N, 0:C],
                                        wmat[:, c * N:(c + 1) * N],
                                        xc[:, c * C:(c + 1) * C],
                                        start=(c == 0), stop=(c == KD - 1))
                    ins.then_inc(sem["s_proj"], 1)

            for m in range(KH):
                if m == 8:
                    em("pe", _proj)
                s_id = slab_idx(it, "h1", m)
                g = gidx(it, "mm2", m)
                def _mm2(pe, m=m, s_id=s_id, g=g, it=it):
                    if m == 0:
                        pe.wait_ge(sem["s_ghl"], KH * it)
                    pe.wait_ge(sem["s_wld"], 16 * min(s_id + 2, NSLAB + 1))
                    pa_wait(pe, g)
                    for k in range(KH):
                        ins = pe.matmul(pa_ap[g % 4][:, 0:2 * C],
                                        wsl[s_id % NSL][:, k * 128:(k + 1) * 128],
                                        ghl[:, k * 2 * C:(k + 1) * 2 * C],
                                        start=(k == 0), stop=(k == KH - 1))
                    ins.then_inc(sem["s_pa"], 1)
                em("pe", _mm2)
            for m in range(KH):
                s_id = slab_idx(it, "h2", m)
                g = gidx(it, "mm3", m)
                def _mm3(pe, m=m, s_id=s_id, g=g, it=it):
                    if m == 0:
                        pe.wait_ge(sem["s_bhl"], KH * it)
                    pe.wait_ge(sem["s_wld"], 16 * min(s_id + 2, NSLAB + 1))
                    pa_wait(pe, g)
                    for k in range(KH):
                        ins = pe.matmul(pa_ap[g % 4][:, 0:2 * C],
                                        wsl[s_id % NSL][:, k * 128:(k + 1) * 128],
                                        bhl[:, k * 2 * C:(k + 1) * 2 * C],
                                        start=(k == 0), stop=(k == KH - 1))
                    ins.then_inc(sem["s_pa"], 1)
                em("pe", _mm3)
            for m in range(KD):
                s_id = slab_idx(it, "w3", m)
                g = gidx(it, "mm4", m)
                def _mm4(pe, m=m, s_id=s_id, g=g, it=it):
                    if m == 0:
                        pe.wait_ge(sem["s_chl"], KH * it)
                        pe.wait_ge(sem["s_ssm"], it)
                    pe.wait_ge(sem["s_wld"], 16 * min(s_id + 2, NSLAB + 1))
                    pa_wait(pe, g)
                    for k in range(KH):
                        ins = pe.matmul(pa_ap[g % 4][:, 0:2 * C],
                                        wsl[s_id % NSL][:, k * 128:(k + 1) * 128],
                                        ghl[:, k * 2 * C:(k + 1) * 2 * C],
                                        start=(k == 0), stop=(k == KH - 1))
                    ins.then_inc(sem["s_pa"], 1)
                    pbu = 4 * (it - 1) + m // 2
                    if pbu >= 1:
                        pe.wait_ge(sem[f"s_pbf{m % 2}"], pbu)
                    pe.matmul(pb_ap[m % 2][:, 0:C],
                              wout_sb[:, m * 128:(m + 1) * 128],
                              chS, start=True, stop=True).then_inc(sem["s_pb"], 1)
                em("pe", _mm4)

        if do_ffn:
            for m in range(KH):
                ga1 = gidx(it, "a1", m)
                ga2 = gidx(it, "a2", m)
                def _gev(v, m=m, ga1=ga1, ga2=ga2, it=it):
                    if m >= 2:
                        v.wait_ge(sem["s_t1"], 96 * (it - 1) + m - 1)
                    v.wait_ge(sem["s_ha"], ga1 + 1)
                    v.tensor_add(asum[m % 2], pa_ap[ga1 % 4][:, 0:C],
                                 chalf[ga1 % 2]).then_inc(sem[f"s_paf{ga1 % 4}"], 1)
                    v.wait_ge(sem["s_ha"], ga2 + 1)
                    v.tensor_add(t2b[m % 2], pa_ap[ga2 % 4][:, 0:C],
                                 chalf[ga2 % 2]).then_inc(sem[f"s_paf{ga2 % 4}"], 1)
                    v.wait_ge(sem["s_t1"], 96 * (it - 1) + m + 1)
                    v.scalar_tensor_tensor(tmpc, t1b[m % 2], s2_sb[:, m:m + 1],
                                           t2b[m % 2], OP.mult, OP.mult)
                    hi = ghl[:, m * 2 * C:m * 2 * C + C]
                    lo = ghl[:, m * 2 * C + C:(m + 1) * 2 * C]
                    v.tensor_copy(hi, tmpc)
                    v.tensor_tensor(t2b[m % 2], tmpc, hi, OP.subtract)
                    if m == KH - 1:
                        v.tensor_copy(lo, t2b[m % 2])
                        v.drain().then_inc(sem["s_ghl"], 1)
                    else:
                        v.tensor_copy(lo, t2b[m % 2]).then_inc(sem["s_ghl"], 1)
                em("dve", _gev)

        if do_ln:
            def _var(v, i=i):
                v.wait_ge(sem["s_stfr"], 2 * i + 2)
                v.tensor_mul(varsb, musb, musb)
                v.tensor_sub(varsb, e2sb, varsb)
                v.tensor_scalar_add(varsb, varsb, EPS_LN)
                v.reciprocal(varsb, varsb).then_inc(sem["s_var"], 1)
            em("dve", _var)
            def _xn(v, i=i, ib=ib):
                v.wait_ge(sem["s_stat"], 4 * i + 4)
                if i >= 2:
                    v.wait_ge(sem["s_pa"], gidx(i - 1, "a2", KH - 1) + 1)
                for c in range(KD):
                    xch = x_buf[ib][:, c * C:(c + 1) * C]
                    v.tensor_sub(tmpc, xch, pc_ap[:, 0:C])
                    v.tensor_mul(tmpc, tmpc, pd_ap[:, 0:C])
                    hi = xnhl[ib][:, c * 2 * C:c * 2 * C + C]
                    lo = xnhl[ib][:, c * 2 * C + C:(c + 1) * 2 * C]
                    v.tensor_copy(hi, tmpc)
                    v.tensor_tensor(tmpd, tmpc, hi, OP.subtract)
                    v.tensor_copy(lo, tmpd)
                    v.tensor_copy(stash[i % 3][:, c:c + 1], tmpc[:, C - 1:C])
                v.drain().then_inc(sem["s_lnfr"], 1)
            em("dve", _xn)

        if do_ffn:
            def _conv(v, it=it, j=j, jb=jb, ib=ib,
                      first=first_of_seq, last=last_of_seq):
                v.wait_ge(sem["s_proj"], 3 * (it - 1))
                for c in range(KD):
                    hi = xnhl[jb][:, c * 2 * C:c * 2 * C + C]
                    lo = xnhl[jb][:, c * 2 * C + C:(c + 1) * 2 * C]
                    if first:
                        v.memset(halo[:, 0:1], 0.0)
                    else:
                        v.tensor_copy(halo[:, 0:1], stash[(j - 1) % 3][:, c:c + 1])
                    if last:
                        v.memset(halo[:, 1 + C:3 + C], 0.0)
                    else:
                        nhl = xnhl[ib]
                        v.tensor_tensor(halo[:, 1 + C:3 + C],
                                        nhl[:, c * 2 * C:c * 2 * C + 2],
                                        nhl[:, c * 2 * C + C:c * 2 * C + C + 2],
                                        OP.add)
                    v.tensor_tensor(halo[:, 1:1 + C], hi, lo, OP.add)
                    xcc = xc[:, c * C:(c + 1) * C]
                    v.tensor_scalar_mul(xcc, halo[:, 0:C], cw_sb[:, c * 4:c * 4 + 1])
                    for k in range(1, 4):
                        v.scalar_tensor_tensor(xcc, halo[:, k:k + C],
                                               cw_sb[:, c * 4 + k:c * 4 + k + 1],
                                               xcc, OP.mult, OP.add)
                    if last:
                        v.memset(xc[:, (c + 1) * C - 1:(c + 1) * C], 0.0)
                        v.tensor_scalar_add(xc[:, c * C:(c + 1) * C - 1],
                                            xc[:, c * C:(c + 1) * C - 1],
                                            cb_sb[:, c:c + 1])
                    else:
                        v.tensor_scalar_add(xcc, xcc, cb_sb[:, c:c + 1])
                    v.drain().then_inc(sem["s_xc"], 1)
            em("dve", _conv)

            def _ssm1(v, it=it):
                v.wait_ge(sem["s_proj"], 3 * (it - 1) + 1)
                v.tensor_scalar_add(d32, pc_ap[0:N, 0:C],
                                    b_d).then_inc(sem["s_projf"], 1)
                v.tensor_scalar_mul(sp1, d32, -1.0)
                v.tensor_tensor(sp1, sp1, d32, OP.max).then_inc(sem["s_d1"], 1)
                v.tensor_relu(sp3, d32)
                v.wait_ge(sem["s_proj"], 3 * (it - 1) + 2)
                v.tensor_scalar_add(b32, pc_ap[0:N, 0:C],
                                    b_b).then_inc(sem["s_projf"], 1)
                v.wait_ge(sem["s_proj"], 3 * (it - 1) + 3)
                v.tensor_scalar_add(c32, pc_ap[0:N, 0:C],
                                    b_c).then_inc(sem["s_projf"], 1)
                v.wait_ge(sem["s_a1"], 2 * it + 1)
                v.tensor_add(sp3, sp3, sp2)
                v.tensor_scalar_mul(dA, sp3, a_sb).then_inc(sem["s_d1"], 1)
            def _ssm2(v, it=it, first=first_of_seq):
                v.tensor_mul(tay, dA, dA)
                v.tensor_scalar(sp3, dA, 0.5, 1.0, OP.mult, OP.add)
                v.scalar_tensor_tensor(tay, tay, 1.0 / 6.0, sp3, OP.mult, OP.add)
                v.wait_ge(sem["s_a1"], 2 * it + 2)
                v.tensor_scalar_add(sp3, abar, -1.0)
                v.tensor_scalar_mul(sp3, sp3, inva_sb)
                v.tensor_sub(tay, tay, sp3)
                v.scalar_tensor_tensor(sp3, tay, mask_sb, sp3, OP.mult, OP.add)
                v.tensor_mul(bbar, sp3, b32)
                if first:
                    v.memset(carry, 0.0)
                    v.drain()
                v.tensor_tensor_scan(hS, abar, bbar, carry, OP.mult, OP.add)
                v.drain()
                v.tensor_copy(carry, hS[:, C - 1:C])
                v.drain()
                if it >= 2:
                    v.wait_ge(sem["s_pb"], KD * (it - 1))
                v.tensor_mul(chS, c32, hS)
                v.drain().then_inc(sem["s_ssm"], 1)

            for m in range(KH):
                if m == 4:
                    em("dve", _ssm1)
                g = gidx(it, "mm2", m)
                def _bspl(v, m=m, g=g, it=it):
                    if m >= 2:
                        v.wait_ge(sem["s_t1"], 96 * (it - 1) + 32 + m - 1)
                    v.wait_ge(sem["s_ha"], g + 1)
                    v.tensor_add(bsum[m % 2], pa_ap[g % 4][:, 0:C],
                                 chalf[g % 2]).then_inc(sem[f"s_paf{g % 4}"], 1)
                    v.wait_ge(sem["s_t1"], 96 * (it - 1) + 32 + m + 1)
                    hi = bhl[:, m * 2 * C:m * 2 * C + C]
                    lo = bhl[:, m * 2 * C + C:(m + 1) * 2 * C]
                    v.tensor_copy(hi, btb[m % 2])
                    v.tensor_tensor(tmpc, btb[m % 2], hi, OP.subtract)
                    if m == KH - 1:
                        v.tensor_copy(lo, tmpc)
                        v.drain().then_inc(sem["s_bhl"], 1)
                    else:
                        v.tensor_copy(lo, tmpc).then_inc(sem["s_bhl"], 1)
                em("dve", _bspl)
            em("dve", _ssm2)
            for m in range(KH):
                g = gidx(it, "mm3", m)
                def _cspl(v, m=m, g=g, it=it):
                    u = 32 + m
                    v.wait_ge(sem["s_t1"], 96 * (it - 1) + 32 + u - 1)
                    v.wait_ge(sem["s_ha"], g + 1)
                    v.tensor_add(bsum[m % 2], pa_ap[g % 4][:, 0:C],
                                 chalf[g % 2]).then_inc(sem[f"s_paf{g % 4}"], 1)
                    v.wait_ge(sem["s_t1"], 96 * (it - 1) + 64 + m + 1)
                    hi = ghl[:, m * 2 * C:m * 2 * C + C]
                    lo = ghl[:, m * 2 * C + C:(m + 1) * 2 * C]
                    v.tensor_copy(hi, btb[m % 2])
                    v.tensor_tensor(tmpc, btb[m % 2], hi, OP.subtract)
                    if m == KH - 1:
                        v.tensor_copy(lo, tmpc)
                        v.drain().then_inc(sem["s_chl"], 1)
                    else:
                        v.tensor_copy(lo, tmpc).then_inc(sem["s_chl"], 1)
                em("dve", _cspl)
            for m in range(KD):
                g = gidx(it, "mm4", m)
                def _out(v, m=m, g=g, it=it, jb=jb):
                    if m == 0 and it >= 2:
                        v.wait_ge(sem["s_y"], 16 * (it - 1))
                    v.wait_ge(sem["s_ha"], g + 1)
                    ych = y_sb[:, m * C:(m + 1) * C]
                    v.tensor_add(ych, pa_ap[g % 4][:, 0:C], chalf[g % 2]
                                 ).then_inc(sem[f"s_paf{g % 4}"], 1)
                    v.tensor_scalar_mul(ych, ych, s3_sb[:, m:m + 1])
                    v.wait_ge(sem["s_pb"], KD * (it - 1) + m + 1)
                    v.tensor_add(ych, ych, pb_ap[m % 2][:, 0:C]
                                 ).then_inc(sem[f"s_pbf{m % 2}"], 1)
                    v.tensor_add(ych, ych, x_buf[jb][:, m * C:(m + 1) * C])
                    ins = v.tensor_scalar_add(ych, ych, bout_sb[:, m:m + 1])
                    if m == KD - 1:
                        ins.then_inc(sem["s_out"], 1)
                em("dve", _out)

    em("act", lambda a: a.wait_ge(sem["s_y"], 16 * nt))
    em("sy", lambda q: q.wait_ge(sem["s_y"], 16 * nt))

    dbg_tensors = {}
    if debug and nt == 0:
        dbg_tensors = dict(
            w12qF=w12q, h1qF=h1q, h2qF=h2q, w3qF=w3q,
            win_dd=win_d, win_bb=win_b, win_cc=win_c, a_s=a_sb,
            mask_s=mask_sb, inva_s=inva_sb, bd_s=b_d, bb_s=b_b, bc_s=b_c,
            cw_s=cw_sb, cb_s=cb_sb, s1_s=s1_sb, s3_s=s3_sb, wout_s=wout_sb,
            bout_s=bout_sb, tb1_s=tb1, tb2_s=tb2, tn1_s=tn1, tn2_s=tn2,
        )
    elif debug:
        dbg_tensors = dict(
            x0=x_buf[0], x1=x_buf[1], xnhl0=xnhl[0], xnhl1=xnhl[1], sq=sq,
            xc=xc, d32=d32, b32=b32, c32=c32, sp1=sp1, sp2=sp2,
            sp3=sp3, dA=dA, abar=abar, tay=tay, bbar=bbar, hS=hS, chS=chS,
            carry=carry, ghl=ghl, bhl=bhl, y_sbuf=y_sb, t1b0=t1b[0],
            t2b0=t2b[0], btb0=btb[0], musb=musb, e2sb=e2sb, varsb=varsb,
            issb=issb, stash0=stash[0], stash1=stash[1], stash2=stash[2],
            win_dd=win_d, win_bb=win_b, win_cc=win_c, a_s=a_sb,
            mask_s=mask_sb, inva_s=inva_sb, bd_s=b_d, bb_s=b_b, bc_s=b_c,
            cw_s=cw_sb, cb_s=cb_sb, s1_s=s1_sb, s3_s=s3_sb, wsl0=wsl[0],
            wsl1=wsl[1], wsl2=wsl[2], wsl3=wsl[3], wout_s=wout_sb,
            bout_s=bout_sb, w12qF=w12q, h1qF=h1q, h2qF=h2q, w3qF=w3q,
        )

    dbg_out = {}
    if debug:
        for k, ap in dbg_tensors.items():
            shp = list(ap.shape)
            sh = [shp[0], int(np.prod(shp[1:]))] if len(shp) > 1 else [shp[0], 1]
            dbg_out[k] = nc.dram_tensor("dbg_" + k, sh, ap.tensor.dtype,
                                        kind="ExternalOutput").ap()

    with nc.Block() as blk:
        @blk.sync
        def _(q):
            for fn in pro["sy"]:
                fn(q)
        @blk.gpsimd
        def _(g):
            for fn in pro["gp"]:
                fn(g)
        @blk.vector
        def _(v):
            for fn in pro["dve"]:
                fn(v)
        @blk.scalar
        def _(a):
            for fn in pro["act"]:
                fn(a)

    with nc.Block() as blk:
        @blk.sync
        def _(q):
            for fn in ops["sy"]:
                fn(q)
        @blk.tensor
        def _(pe):
            for fn in ops["pe"]:
                fn(pe)
        @blk.vector
        def _(v):
            for fn in ops["dve"]:
                fn(v)
        @blk.scalar
        def _(a):
            for fn in ops["act"]:
                fn(a)

    if debug:
        dsem = nc.alloc_semaphore(name="dsem")
        with nc.Block() as blk:
            @blk.sync
            def _(q):
                n = 0
                for k, ap in dbg_tensors.items():
                    q.dma_start(out=dbg_out[k], in_=ap).then_inc(dsem, 16)
                    n += 1
                q.wait_ge(dsem, 16 * n)

    return nc


def _host_prep(inputs):
    f = np.float32
    att_x = np.asarray(inputs["L0_attn_x"], dtype=f)
    w1T = np.ascontiguousarray(np.asarray(inputs["L0_ffn_w1_w"], dtype=f).T)
    w2T = np.ascontiguousarray(np.asarray(inputs["L0_ffn_w2_w"], dtype=f).T)
    h1T = np.ascontiguousarray(np.asarray(inputs["L0_ffn_h1_w"], dtype=f).T)
    h2T = np.ascontiguousarray(np.asarray(inputs["L0_ffn_h2_w"], dtype=f).T)
    w3T = np.ascontiguousarray(np.asarray(inputs["L0_ffn_w3_w"], dtype=f).T)

    def blk(wT, km, mm):
        return np.ascontiguousarray(
            wT.reshape(km, 128, mm, 128).transpose(2, 1, 0, 3).reshape(mm, 128, km * 128))

    w1b = blk(w1T, KD, KH)
    w2b = blk(w2T, KD, KH)
    w12s = np.ascontiguousarray(np.concatenate([w1b, w2b], axis=2))
    h1s = blk(h1T, KH, KH)
    h2s = blk(h2T, KH, KH)
    w3s = blk(w3T, KH, KD)

    win = np.asarray(inputs["L0_attn_Win"], dtype=f)
    def winblk(w):
        return np.ascontiguousarray(w.reshape(KD, 128, N).transpose(1, 0, 2).reshape(128, KD * N))
    wind_h = winblk(win[:, :N])
    winbe_h = winblk(win[:, N:3 * N:2])
    winbo_h = winblk(win[:, N + 1:3 * N:2])
    wince_h = winblk(win[:, 3 * N:5 * N:2])
    winco_h = winblk(win[:, 3 * N + 1:5 * N:2])
    b_in = np.asarray(inputs["L0_attn_b_in"], dtype=f)
    bd_h = np.ascontiguousarray(b_in[:N].reshape(N, 1))
    bbe_h = np.ascontiguousarray(b_in[N:3 * N:2].reshape(N, 1))
    bbo_h = np.ascontiguousarray(b_in[N + 1:3 * N:2].reshape(N, 1))
    bce_h = np.ascontiguousarray(b_in[3 * N:5 * N:2].reshape(N, 1))
    bco_h = np.ascontiguousarray(b_in[3 * N + 1:5 * N:2].reshape(N, 1))
    wout_h = np.ascontiguousarray(np.asarray(inputs["L0_attn_Wout"], dtype=f))
    bout_h = np.ascontiguousarray(np.asarray(inputs["L0_attn_b_out"], dtype=f).reshape(KD, 128).T)
    alog_h = np.ascontiguousarray(np.asarray(inputs["L0_attn_A_log"], dtype=f).reshape(N, 1))
    cw = np.asarray(inputs["L0_attn_conv_w"], dtype=f).reshape(D, 4)
    cw_h = np.ascontiguousarray(cw.reshape(KD, 128, 4).transpose(1, 0, 2).reshape(128, KD * 4))
    cb_h = np.ascontiguousarray(np.asarray(inputs["L0_attn_conv_b"], dtype=f).reshape(KD, 128).T)

    def sc(v, km):
        return np.ascontiguousarray(np.asarray(v, dtype=f).reshape(km, 128).T)

    shared = dict(
        w12s=w12s, h1s=h1s, h2s=h2s, w3s=w3s, wind_h=wind_h, winbe_h=winbe_h,
        winbo_h=winbo_h, wince_h=wince_h, winco_h=winco_h, bd_h=bd_h,
        bbe_h=bbe_h, bbo_h=bbo_h, bce_h=bce_h, bco_h=bco_h,
        wout_h=wout_h, bout_h=bout_h, alog_h=alog_h, cw_h=cw_h, cb_h=cb_h,
        s1_h=sc(inputs["L0_ffn_w1_s"], KH), s2_h=sc(inputs["L0_ffn_w2_s"], KH),
        sh1_h=sc(inputs["L0_ffn_h1_s"], KH), sh2_h=sc(inputs["L0_ffn_h2_s"], KH),
        s3_h=sc(inputs["L0_ffn_w3_s"], KD),
    )
    in_maps = []
    for c in range(NCORE):
        pair = att_x[SEQ_PER_CORE * c:SEQ_PER_CORE * (c + 1)]
        xTc = np.ascontiguousarray(pair.transpose(2, 0, 1).reshape(D, SEQ_PER_CORE * L))
        m = dict(shared)
        m["xT"] = xTc
        in_maps.append(m)
    return in_maps


def kernel(**inputs):
    in_maps = _host_prep(inputs)
    nc = build_program(NT)
    res = run_bass_kernel_spmd(nc, in_maps, core_ids=list(range(NCORE)))
    out = np.empty((B, L, D), dtype=np.float32)
    for c in range(NCORE):
        yT = res.results[c]["yT"]
        pair = yT.reshape(D, SEQ_PER_CORE, L).transpose(1, 2, 0)
        out[SEQ_PER_CORE * c:SEQ_PER_CORE * (c + 1)] = pair
    return out
